# version 05
# baseline (speedup 1.0000x reference)
import numpy as np
import ml_dtypes
from contextlib import ExitStack

import concourse.bass as bass
import concourse.mybir as mybir
from concourse.bass_utils import run_bass_kernel_spmd
from concourse.bass_core_rust import IndirectOffsetOnAxis

F32 = mybir.dt.float32
BF16 = mybir.dt.bfloat16
I32 = mybir.dt.int32
AF = mybir.ActivationFunctionType
ALU = mybir.AluOpType
AX = mybir.AxisListType
BF = ml_dtypes.bfloat16

D = 2048
T = 8192
NCORES = 8
TL = T // NCORES
NCH = D // 128
HID = 1024
NEXP = 9
HLOC = 4
HD = 64
EPS = 1e-5

PHASES = ("ln1", "ffn1", "ln2", "qkv", "attn", "wo", "ln3", "ffn2")


class Ctr:
    def __init__(self, handle):
        self.h = handle
        self.n = 0


def build_nc(upto="ffn2", taps=(), ffn_mode="full"):
    nc = bass.Bass("TRN2", target_bir_lowering=False)

    x_in = nc.dram_tensor("x_in", [D, TL], F32, kind="ExternalInput").ap()
    wqkvT_d = nc.dram_tensor("wqkvT", [D, 384], F32, kind="ExternalInput").ap()
    woT_d = nc.dram_tensor("woT", [D, D], F32, kind="ExternalInput").ap()
    w1T_d = nc.dram_tensor("w1T", [NEXP, D, HID], F32, kind="ExternalInput").ap()
    w3T_d = nc.dram_tensor("w3T", [NEXP, D, HID], F32, kind="ExternalInput").ap()
    w2T_d = nc.dram_tensor("w2T", [NEXP, HID, D], F32, kind="ExternalInput").ap()
    gateT_d = nc.dram_tensor("gateT", [D, 8], F32, kind="ExternalInput").ap()
    biasT_d = nc.dram_tensor("biasT", [128, 8], F32, kind="ExternalInput").ap()
    tq_in = nc.dram_tensor("tq_in", [64, 2 * T], BF16, kind="ExternalInput").ap()
    m01_in = nc.dram_tensor("m01_in", [128, 2048], BF16, kind="ExternalInput").ap()
    idbf_in = nc.dram_tensor("idbf_in", [128, 128], BF16, kind="ExternalInput").ap()
    idf32_in = nc.dram_tensor("idf32_in", [128, 128], F32, kind="ExternalInput").ap()
    ones_in = nc.dram_tensor("ones_in", [1, 128], F32, kind="ExternalInput").ap()
    onescol_in = nc.dram_tensor("onescol_in", [128, 1], BF16, kind="ExternalInput").ap()
    oneh_in = nc.dram_tensor("oneh_in", [8, 1024], BF16, kind="ExternalInput").ap()
    woidx_in = nc.dram_tensor("woidx_in", [128, 16], I32, kind="ExternalInput").ap()

    out_x = nc.dram_tensor("out_x", [D, TL], F32, kind="ExternalOutput").ap()
    tap_aps = {}
    if "xln" in taps:
        tap_aps["tap_xln"] = nc.dram_tensor("tap_xln", [128, 16384], BF16,
                                            kind="ExternalOutput").ap()
    if "qp" in taps:
        tap_aps["tap_qp"] = nc.dram_tensor("tap_qp", [256, T], BF16,
                                           kind="ExternalOutput").ap()
    if "kv" in taps:
        tap_aps["tap_kv"] = nc.dram_tensor("tap_kv", [128, T], BF16,
                                           kind="ExternalOutput").ap()
    if "cc2" in taps:
        tap_aps["tap_cc2"] = nc.dram_tensor("tap_cc2", [256, T], BF16,
                                            kind="ExternalOutput").ap()
    if "negub" in taps:
        tap_aps["tap_negub"] = nc.dram_tensor("tap_negub", [128, 1], F32,
                                              kind="ExternalOutput").ap()
        tap_aps["tap_urow"] = nc.dram_tensor("tap_urow", [1, 512], F32,
                                             kind="ExternalOutput").ap()
    if "re" in taps:
        tap_aps["tap_re"] = nc.dram_tensor("tap_re", [128, 8192], BF16,
                                           kind="ExternalOutput").ap()
    if "mfm" in taps:
        tap_aps["tap_mfm"] = nc.dram_tensor("tap_mfm", [8, 1024], BF16,
                                            kind="ExternalOutput").ap()
    if "mbb" in taps:
        tap_aps["tap_mbb"] = nc.dram_tensor("tap_mbb", [128, 1024], BF16,
                                            kind="ExternalOutput").ap()
    if "vaug" in taps:
        tap_aps["tap_vaug"] = nc.dram_tensor("tap_vaug", [128, 65 * 64], BF16,
                                             kind="ExternalOutput").ap()

    cc1_in = nc.dram_tensor("cc1_in", [D, TL], BF16).ap()
    cc1_out = nc.dram_tensor("cc1_out", [NCORES * D, TL], BF16).ap()
    cc2_in = nc.dram_tensor("cc2_in", [256, T], BF16).ap()
    cc2_out = nc.dram_tensor("cc2_out", [D, T], BF16).ap()
    qp_dram = nc.dram_tensor("qp_dram", [256, T], BF16).ap()

    with ExitStack() as st:
        e = st.enter_context
        X = e(nc.sbuf_tensor("X", [128, NCH * TL], F32))
        RA = e(nc.sbuf_tensor("RA", [128, 16384], BF16))
        RB = e(nc.sbuf_tensor("RB", [128, 20480], BF16))
        KV = e(nc.sbuf_tensor("KV", [128, T], BF16))
        RE = e(nc.sbuf_tensor("RE", [128, 8192], BF16))
        VAUG = e(nc.sbuf_tensor("VAUG", [128, 65 * 64], BF16))
        M01S = e(nc.sbuf_tensor("M01S", [128, 2048], BF16))
        ROWS = e(nc.sbuf_tensor("ROWS", [1, 1536], F32))
        MFM = e(nc.sbuf_tensor("MFM", [8, 1024], BF16))
        SEL = e(nc.sbuf_tensor("SEL", [128, 80], F32))
        MBB = e(nc.sbuf_tensor("MBB", [128, 1024], BF16))
        HT = e(nc.sbuf_tensor("HT", [128, 1024], BF16))
        QT = e(nc.sbuf_tensor("QT", [128, 1024], BF16))
        RT = e(nc.sbuf_tensor("RT", [128, 1024], BF16))
        YO = e(nc.sbuf_tensor("YO", [64, 1024], BF16))
        ACCS = e(nc.sbuf_tensor("ACCS", [128, 512], F32))
        QA = e(nc.sbuf_tensor("QA", [128, 4], F32))
        NEGUB = e(nc.sbuf_tensor("NEGUB", [128, 1], F32))
        IDBF = e(nc.sbuf_tensor("IDBF", [128, 128], BF16))
        IDF32 = e(nc.sbuf_tensor("IDF32", [128, 128], F32))
        ONES1 = e(nc.sbuf_tensor("ONES1", [1, 128], F32))
        ONESC = e(nc.sbuf_tensor("ONESC", [128, 1], BF16))
        GATET = e(nc.sbuf_tensor("GATET", [128, 128], BF16))
        BIASB = e(nc.sbuf_tensor("BIASB", [128, 8], F32))
        ONEH = e(nc.sbuf_tensor("ONEH", [8, 1024], BF16))
        WOIDX = e(nc.sbuf_tensor("WOIDX", [128, 16], I32))

        REC = ROWS[0:1, 0:512]
        LNT = None
        UROW = ROWS[0:1, 512:1024]
        PSW = [e(nc.psum_tensor(f"psw{i}", [128, 1024], F32)) for i in range(4)]
        PS = [PSW[i // 2][:, 512 * (i % 2):512 * (i % 2 + 1)] for i in range(8)]

        sX = Ctr(e(nc.semaphore("sX")))
        sS = Ctr(e(nc.semaphore("sS")))
        sW = Ctr(e(nc.semaphore("sW")))
        sA0 = Ctr(e(nc.semaphore("sA0")))
        sA1 = Ctr(e(nc.semaphore("sA1")))
        sPE = Ctr(e(nc.semaphore("sPE")))
        sDVE = Ctr(e(nc.semaphore("sDVE")))
        sACT = Ctr(e(nc.semaphore("sACT")))
        sCC = Ctr(e(nc.semaphore("sCC")))
        block = e(nc.Block())

        prog = {k: [] for k in ("sync", "scalar", "vector", "tensor", "gpsimd")}

        def ew(eng, fn, inc=None, k=1):
            if inc is not None:
                inc.n += k
                def f(en, fn=fn, h=inc.h, k=k):
                    fn(en).then_inc(h, k)
            else:
                def f(en, fn=fn):
                    fn(en)
            prog[eng].append(f)

        def wait(eng, sc, n=None):
            tgt = sc.n if n is None else n
            prog[eng].append(lambda en, h=sc.h, v=tgt: en.wait_ge(h, v))

        def dma(eng, out, in_, sc=None, **kw):
            sc = sc if sc is not None else {"sync": sX, "scalar": sS, "gpsimd": sW}[eng]
            ew(eng, lambda en, o=out, i=in_, kw=kw: en.dma_start(out=o, in_=i, **kw),
               inc=sc, k=16)
            return sc.n

        def fence(eng, inc=None, k=1):
            ew(eng, lambda en: en.drain(), inc=inc, k=k)

        bank_rel = [None] * 8

        def pe_bank(b):
            r = bank_rel[b]
            if r is not None:
                wait("tensor", r[0], r[1])
                bank_rel[b] = None

        def rel_bank(b, sc):
            bank_rel[b] = (sc, sc.n)

        dma("sync", X[:, :].rearrange("p (c t) -> p c t", t=TL),
            x_in.rearrange("(c p) t -> p c t", p=128))
        dma("sync", IDBF[:, :], idbf_in)
        dma("sync", IDF32[:, :], idf32_in)
        dma("sync", ONES1[:, :], ones_in)
        dma("sync", ONESC[:, :], onescol_in)
        dma("sync", BIASB[:, :], biasT_d)
        dma("sync", ONEH[:, :], oneh_in)
        dma("sync", WOIDX[:, :], woidx_in)
        dma("sync", M01S[:, :], m01_in)
        init_sync = sX.n
        dma("gpsimd", GATET[:, :].rearrange("p (k e) -> p k e", e=8),
            gateT_d.rearrange("(k p) e -> p k e", p=128))
        init_gp = sW.n

        def phase_ln():
            wait("vector", sX, init_sync)
            wait("scalar", sX, init_sync)
            wait("scalar", sDVE)
            for b in range(4):
                pe_bank(b)
            pe_after_mm = []
            cslots = [HT, RT]
            qslots = [QT, MBB]
            for c in range(NCH):
                sl = c % 2
                if c >= 2:
                    wait("vector", sPE, pe_after_mm[c - 2])
                    wait("scalar", sPE, pe_after_mm[c - 2])
                ew("vector", lambda en, c=c, sl=sl: en.tensor_copy(
                    cslots[sl][:, :], X[:, 1024 * c:1024 * (c + 1)]), inc=sDVE)
                ew("scalar", lambda en, c=c, sl=sl: en.activation(
                    qslots[sl][:, :], X[:, 1024 * c:1024 * (c + 1)], AF.Square),
                   inc=sACT)
                wait("tensor", sDVE)
                wait("tensor", sACT)
                for h in range(2):
                    ew("tensor", lambda en, c=c, sl=sl, h=h: en.matmul(
                        PS[h][0:1, :], ONESC[:, :],
                        cslots[sl][:, 512 * h:512 * (h + 1)],
                        start=(c == 0), stop=(c == NCH - 1)))
                    ew("tensor", lambda en, c=c, sl=sl, h=h: en.matmul(
                        PS[2 + h][0:1, :], ONESC[:, :],
                        qslots[sl][:, 512 * h:512 * (h + 1)],
                        start=(c == 0), stop=(c == NCH - 1)), inc=sPE)
                pe_after_mm.append(sPE.n)
            pe_stats_done = sPE.n
            r_rstd = ROWS[0:1, 0:512]
            r_mr = ROWS[0:1, 512:1024]
            r_scr = ROWS[0:1, 1024:1536]
            for h in range(2):
                wait("vector", sPE, pe_stats_done)
                ew("vector", lambda en, h=h: en.tensor_scalar_mul(
                    r_scr, PS[h][0:1, :], 1.0 / D))
                fence("vector")
                ew("vector", lambda en: en.tensor_mul(r_mr, r_scr, r_scr))
                ew("vector", lambda en, h=h: en.tensor_scalar_mul(
                    r_rstd, PS[2 + h][0:1, :], 1.0 / D))
                fence("vector")
                ew("vector", lambda en: en.tensor_sub(r_rstd, r_rstd, r_mr))
                fence("vector")
                ew("vector", lambda en: en.tensor_scalar_add(r_rstd, r_rstd, EPS))
                fence("vector", inc=sDVE)
                wait("scalar", sDVE)
                ew("scalar", lambda en: en.activation(r_rstd, r_rstd, AF.Ln))
                fence("scalar")
                ew("scalar", lambda en: en.activation(r_rstd, r_rstd, AF.Exp,
                                                      scale=-0.5))
                fence("scalar", inc=sACT)
                wait("vector", sACT)
                ew("vector", lambda en: en.tensor_mul(r_mr, r_scr, r_rstd))
                fence("vector", inc=sDVE)
                if h == 1:
                    for b in range(4):
                        rel_bank(b, sDVE)
                pe_bank(4)
                pe_bank(5)
                wait("tensor", sDVE)
                ew("tensor", lambda en: en.matmul(PS[4][:, :], ONES1[:, :], r_rstd,
                                                  start=True, stop=True))
                ew("tensor", lambda en: en.matmul(PS[5][:, :], ONES1[:, :], r_mr,
                                                  start=True, stop=True), inc=sPE)
                wait("vector", sPE)
                for c in range(NCH):
                    ew("vector", lambda en, c=c, h=h: en.tensor_mul(
                        ACCS[:, :], X[:, 1024 * c + 512 * h:1024 * c + 512 * (h + 1)],
                        PS[4][:, :]))
                    ew("vector", lambda en, c=c, h=h: en.tensor_sub(
                        RA[:, 1024 * c + 512 * h:1024 * c + 512 * (h + 1)],
                        ACCS[:, :], PS[5][:, :]), inc=sDVE)
                rel_bank(4, sDVE)
                rel_bank(5, sDVE)

        slot_sems = [sA0, sA1]
        slab_guard = {}

        def load_slab13(ex, mg):
            slot = mg % 2
            sc = slot_sems[slot]
            g = slab_guard.pop(("w13", slot), None)
            if g is not None:
                wait("gpsimd", sPE, g)
            dma("gpsimd",
                RB[:, 4096 * slot:4096 * (slot + 1)].rearrange("p (k m) -> p k m",
                                                               m=256),
                w1T_d[ex, :, 256 * mg:256 * (mg + 1)].rearrange("(k p) m -> p k m",
                                                                p=128), sc=sc)
            dma("gpsimd",
                RB[:, 8192 + 4096 * slot:8192 + 4096 * (slot + 1)].rearrange(
                    "p (k m) -> p k m", m=256),
                w3T_d[ex, :, 256 * mg:256 * (mg + 1)].rearrange("(k p) m -> p k m",
                                                               p=128), sc=sc)
            return sc.n

        def load_slab2(ex, dg):
            slot = dg % 2
            sc = slot_sems[slot]
            g = slab_guard.pop(("w2", slot), None)
            if g is not None:
                wait("gpsimd", sPE, g)
            dma("gpsimd",
                RB[:, 16384 + 2048 * slot:16384 + 2048 * (slot + 1)].rearrange(
                    "p (k m) -> p k m", m=256),
                w2T_d[ex, :, 256 * dg:256 * (dg + 1)].rearrange("(k p) m -> p k m",
                                                               p=128), sc=sc)
            return sc.n

        def phase_ffn():
            xln_done = sDVE.n
            wait("tensor", sW, init_gp)
            for tm in range(8):
                pe_bank(6)
                wait("tensor", sDVE, xln_done)
                for k in range(NCH):
                    ew("tensor", lambda en, tm=tm, k=k: en.matmul(
                        PS[6][:, 0:8],
                        RA[:, 1024 * k + 128 * tm:1024 * k + 128 * (tm + 1)],
                        GATET[:, 8 * k:8 * (k + 1)],
                        start=(k == 0), stop=(k == NCH - 1)),
                       inc=(sPE if k == NCH - 1 else None))
                wait("scalar", sPE)
                ew("scalar", lambda en: en.activation(SEL[:, 0:8], PS[6][:, 0:8],
                                                      AF.Sigmoid))
                fence("scalar", inc=sACT)
                rel_bank(6, sACT)
                wait("vector", sACT)
                ew("vector", lambda en: en.tensor_add(SEL[:, 8:16], SEL[:, 0:8],
                                                      BIASB[:, :]))
                fence("vector")
                ew("vector", lambda en: en.reduce_max(SEL[:, 16:17], SEL[:, 8:16],
                                                      axis=AX.X, negate=True))
                fence("vector", inc=sDVE)
                wait("scalar", sDVE)
                ew("scalar", lambda en: en.activation(SEL[:, 24:32], SEL[:, 8:16],
                                                      AF.Sign, bias=SEL[:, 16:17]))
                fence("scalar", inc=sACT)
                wait("vector", sACT)
                ew("vector", lambda en: en.tensor_single_scalar(
                    SEL[:, 24:32], SEL[:, 24:32], 1.0, ALU.add))
                fence("vector")
                ew("vector", lambda en: en.scalar_tensor_tensor(
                    SEL[:, 32:40], SEL[:, 24:32], -1e30, SEL[:, 8:16],
                    ALU.mult, ALU.add))
                fence("vector")
                ew("vector", lambda en: en.reduce_max(SEL[:, 19:20], SEL[:, 32:40],
                                                      axis=AX.X, negate=True))
                ew("vector", lambda en: en.reduce_sum(SEL[:, 17:18], SEL[:, 24:32],
                                                      axis=AX.X))
                fence("vector")
                ew("vector", lambda en: en.tensor_single_scalar(
                    SEL[:, 18:19], SEL[:, 17:18], 1.5, ALU.is_lt))
                fence("vector", inc=sDVE)
                wait("scalar", sDVE)
                ew("scalar", lambda en: en.activation(SEL[:, 40:48], SEL[:, 32:40],
                                                      AF.Sign, bias=SEL[:, 19:20]))
                fence("scalar", inc=sACT)
                wait("vector", sACT)
                ew("vector", lambda en: en.tensor_single_scalar(
                    SEL[:, 40:48], SEL[:, 40:48], 1.0, ALU.add))
                fence("vector", inc=sDVE)
                wait("scalar", sDVE)
                ew("scalar", lambda en: en.activation(SEL[:, 48:56], SEL[:, 40:48],
                                                      AF.Copy, scale=SEL[:, 18:19]))
                fence("scalar", inc=sACT)
                wait("vector", sACT)
                ew("vector", lambda en: en.tensor_add(SEL[:, 56:64], SEL[:, 24:32],
                                                      SEL[:, 48:56]))
                fence("vector")
                ew("vector", lambda en: en.tensor_mul(SEL[:, 64:72], SEL[:, 56:64],
                                                      SEL[:, 0:8]))
                fence("vector", inc=sDVE)
                pe_bank(7)
                wait("tensor", sDVE)
                ew("tensor", lambda en: en.transpose(PS[7][0:8, 0:128], SEL[:, 64:72],
                                                     IDF32[:, :]), inc=sPE)
                wait("vector", sPE)
                ew("vector", lambda en, tm=tm: en.tensor_copy(
                    MFM[0:8, 128 * tm:128 * (tm + 1)], PS[7][0:8, 0:128]))
                fence("vector", inc=sDVE)
                rel_bank(7, sDVE)
            sel_done = sDVE.n
            if ffn_mode == "sel":
                return
            nexp_run = {"mbb": 0, "a1": 1, "ab1": 1, "a1slab": 1,
                        "a1mm": 1, "a1mm1": 1, "a1mmne": 1,
                        "a1slab3": 1}.get(ffn_mode, NEXP)

            ex_list = ([8] if ffn_mode == "sh" else
                       [0] if ffn_mode == "e0" else list(range(nexp_run)))
            for ex in ex_list:
                routed = ex < 8
                if routed:
                    pe_bank(4)
                    pe_bank(5)
                    wait("tensor", sDVE, sel_done)
                    for h in range(2):
                        ew("tensor", lambda en, ex=ex, h=h: en.matmul(
                            PS[4 + h][:, :], ONEH[:, 128 * ex:128 * (ex + 1)],
                            MFM[:, 512 * h:512 * (h + 1)], start=True, stop=True),
                           inc=(sPE if h == 1 else None))
                    wait("scalar", sPE)
                    for h in range(2):
                        ew("scalar", lambda en, h=h: en.activation(
                            MBB[:, 512 * h:512 * (h + 1)], PS[4 + h][:, :], AF.Copy),
                           inc=sACT)
                    rel_bank(4, sACT)
                    rel_bank(5, sACT)
                mbb_done = sACT.n
                if ffn_mode == "mbb":
                    continue

                slab_ready = {0: load_slab13(ex, 0), 1: load_slab13(ex, 1)}
                if ffn_mode == "a1slab":
                    wait("tensor", slot_sems[0], slab_ready[0])
                    wait("tensor", slot_sems[1], slab_ready[1])
                    continue
                if ffn_mode == "a1slab3":
                    wait("tensor", slot_sems[0], slab_ready[0])
                    wait("tensor", slot_sems[1], slab_ready[1])
                    slab_guard[("w13", 0)] = sPE.n
                    load_slab13(ex, 2)
                    wait("tensor", slot_sems[0])
                    continue
                ngr = 1 if ffn_mode in ("a1mm1", "a1mmne") else 4
                for mg in range(ngr):
                    slot = mg % 2
                    w1s = RB[:, 4096 * slot:4096 * (slot + 1)]
                    w3s = RB[:, 8192 + 4096 * slot:8192 + 4096 * (slot + 1)]
                    wait("tensor", slot_sems[slot], slab_ready[mg])
                    nhalf = 1 if ffn_mode in ("a1mm1", "a1mmne") else 2
                    for half in range(nhalf):
                        bi = (0, 1, 2, 3) if (2 * mg + half) % 2 == 0 else (4, 5, 6, 7)
                        for b in bi:
                            pe_bank(b)
                        for k in range(NCH):
                            for m in range(2):
                                ew("tensor", lambda en, k=k, m=m, half=half, w1s=w1s,
                                   bi=bi: en.matmul(
                                       PS[bi[m]][:, :],
                                       w1s[:, 256 * k + 128 * m:256 * k + 128 * (m + 1)],
                                       RA[:, 1024 * k + 512 * half:
                                          1024 * k + 512 * (half + 1)],
                                       start=(k == 0), stop=(k == NCH - 1)))
                                ew("tensor", lambda en, k=k, m=m, half=half, w3s=w3s,
                                   bi=bi: en.matmul(
                                       PS[bi[2 + m]][:, :],
                                       w3s[:, 256 * k + 128 * m:256 * k + 128 * (m + 1)],
                                       RA[:, 1024 * k + 512 * half:
                                          1024 * k + 512 * (half + 1)],
                                       start=(k == 0), stop=(k == NCH - 1)),
                                   inc=(sPE if (k == NCH - 1 and m == 1) else None))
                        wait("vector", sPE)
                        wait("scalar", sPE)
                        if ffn_mode == "a1mmne":
                            ew("vector", lambda en: en.engine_nop(), inc=sDVE)
                            for b in bi:
                                bank_rel[b] = None
                            continue
                        if ffn_mode in ("a1mm", "a1mm1"):
                            for m in range(2):
                                hc = 2 * mg + m
                                hdst = RE[:, 1024 * hc + 512 * half:
                                          1024 * hc + 512 * (half + 1)]
                                ew("vector", lambda en, bi=bi, m=m, hdst=hdst:
                                   en.tensor_copy(hdst, PS[bi[m]][:, :]), inc=sDVE)
                            for b in bi:
                                rel_bank(b, sDVE)
                            continue
                        if routed:
                            wait("vector", sACT, mbb_done)
                        for m in range(2):
                            hc = 2 * mg + m
                            hdst = RE[:, 1024 * hc + 512 * half:
                                      1024 * hc + 512 * (half + 1)]
                            qtm = QT[:, 512 * m:512 * (m + 1)]
                            htm = HT[:, 512 * m:512 * (m + 1)]
                            if routed:
                                ew("vector", lambda en, bi=bi, m=m, half=half, htm=htm:
                                   en.tensor_mul(htm, PS[bi[m]][:, :],
                                                 MBB[:, 512 * half:512 * (half + 1)]),
                                   inc=sDVE)
                                wait("scalar", sDVE)
                                ew("scalar", lambda en, qtm=qtm, htm=htm:
                                   en.activation(qtm, htm, AF.Silu), inc=sACT)
                                wait("vector", sACT)
                                ew("vector", lambda en, bi=bi, m=m, half=half, htm=htm:
                                   en.tensor_mul(htm, PS[bi[2 + m]][:, :],
                                                 MBB[:, 512 * half:512 * (half + 1)]))
                                ew("vector", lambda en, qtm=qtm, htm=htm, hdst=hdst:
                                   en.tensor_mul(hdst, qtm, htm), inc=sDVE)
                            else:
                                ew("scalar", lambda en, bi=bi, m=m, qtm=qtm:
                                   en.activation(qtm, PS[bi[m]][:, :], AF.Silu),
                                   inc=sACT)
                                wait("vector", sACT)
                                ew("vector", lambda en, bi=bi, m=m, qtm=qtm, hdst=hdst:
                                   en.tensor_mul(hdst, qtm, PS[bi[2 + m]][:, :]),
                                   inc=sDVE)
                        for b in bi:
                            rel_bank(b, sDVE)
                    slab_guard[("w13", slot)] = sPE.n
                    if mg + 2 < 4:
                        slab_ready[mg + 2] = load_slab13(ex, mg + 2)
                h_done = sDVE.n
                if ffn_mode in ("a1", "a1mm", "a1mm1", "a1mmne", "a1slab3"):
                    continue

                w2_ready = {0: load_slab2(ex, 0), 1: load_slab2(ex, 1)}
                for dg in range(8):
                    slot = dg % 2
                    w2s = RB[:, 16384 + 2048 * slot:16384 + 2048 * (slot + 1)]
                    wait("tensor", slot_sems[slot], w2_ready[dg])
                    wait("tensor", sDVE, h_done)
                    for half in range(2):
                        bi = (0, 1) if (2 * dg + half) % 2 == 0 else (2, 3)
                        for b in bi:
                            pe_bank(b)
                        for k in range(8):
                            for m in range(2):
                                ew("tensor", lambda en, k=k, m=m, half=half, w2s=w2s,
                                   bi=bi: en.matmul(
                                       PS[bi[m]][:, :],
                                       w2s[:, 256 * k + 128 * m:256 * k + 128 * (m + 1)],
                                       RE[:, 1024 * k + 512 * half:
                                          1024 * k + 512 * (half + 1)],
                                       start=(k == 0), stop=(k == 7)),
                                   inc=(sPE if (k == 7 and m == 1) else None))
                        wait("vector", sPE)
                        for m in range(2):
                            dc = 2 * dg + m
                            xc = X[:, 1024 * dc + 512 * half:1024 * dc + 512 * (half + 1)]
                            ew("vector", lambda en, bi=bi, m=m, xc=xc:
                               en.scalar_tensor_tensor(xc, PS[bi[m]][:, :], 0.5, xc,
                                                       ALU.mult, ALU.add), inc=sDVE)
                        for b in bi:
                            rel_bank(b, sDVE)
                    slab_guard[("w2", slot)] = sPE.n
                    if dg + 2 < 8:
                        w2_ready[dg + 2] = load_slab2(ex, dg + 2)

        def phase_qkv():
            xln_done = sDVE.n
            wait("sync", sDVE, xln_done)
            dma("sync", cc1_in.rearrange("(c p) t -> p c t", p=128),
                RA[:, :].rearrange("p (c t) -> p c t", t=TL))
            cc1_staged = sX.n
            wait("gpsimd", sX, cc1_staged)
            ew("gpsimd", lambda en: en.collective_compute(
                "AllGather", ALU.bypass, replica_groups=[list(range(NCORES))],
                ins=[cc1_in], outs=[cc1_out]), inc=sCC)
            ag1_done = sCC.n
            g0 = slab_guard.pop(("w13", 0), None)
            g1 = slab_guard.pop(("w13", 1), None)
            g2a = slab_guard.pop(("w2", 0), None)
            g2b = slab_guard.pop(("w2", 1), None)
            pe_fence = max(x for x in (g0, g1, g2a, g2b, 0) if x is not None)
            wait("sync", sPE, pe_fence)
            dma("sync", RB[0:64, 0:16384], tq_in)
            tq_done = sX.n
            wait("gpsimd", sPE, pe_fence)
            dma("gpsimd", RE[:, 0:6144].rearrange("p (k f) -> p k f", f=384),
                wqkvT_d.rearrange("(k p) f -> p k f", p=128))
            wqkv_done = sW.n

            xg_view = cc1_out.rearrange("(b k p) t -> b p k t", k=NCH, p=128)
            xg_loaded = []
            wait("sync", sCC, ag1_done)
            dma("sync", RA[:, 0:8192].rearrange("p (k t) -> p k t", t=512),
                xg_view[0, :, :, 0:512])
            xg_loaded.append(sX.n)
            pe_after_n = []
            qslot_dve = [0, 0]
            qsq_act = [0, 0]
            hx_dma = [0, 0]
            qa_q = False
            qa_k = False
            for n in range(16):
                slot = n % 2
                if n + 1 < 16:
                    if n >= 1:
                        wait("sync", sPE, pe_after_n[n - 1])
                    nn = n + 1
                    dma("sync",
                        RA[:, 8192 * (nn % 2):8192 * (nn % 2) + 8192].rearrange(
                            "p (k t) -> p k t", t=512),
                        xg_view[nn // 2, :, :, 512 * (nn % 2):512 * (nn % 2) + 512])
                    xg_loaded.append(sX.n)
                bset = (0, 1, 2) if n % 2 == 0 else (3, 4, 5)
                wait("tensor", sX, xg_loaded[n])
                if n == 0:
                    wait("tensor", sW, wqkv_done)
                for m in range(3):
                    pe_bank(bset[m])
                    for k in range(NCH):
                        ew("tensor", lambda en, m=m, k=k, slot=slot, bset=bset:
                           en.matmul(
                               PS[bset[m]][:, :],
                               RE[:, 384 * k + 128 * m:384 * k + 128 * (m + 1)],
                               RA[:, 8192 * slot + 512 * k:8192 * slot + 512 * (k + 1)],
                               start=(k == 0), stop=(k == NCH - 1)),
                           inc=(sPE if k == NCH - 1 else None))
                pe_after_n.append(sPE.n)
                wait("scalar", sPE, pe_after_n[n])
                if n == 0:
                    wait("scalar", sX, tq_done)
                    wait("vector", sX, tq_done)
                for m in range(3):
                    qslot = (3 * n + m) % 2
                    qte = QT[0:64, 512 * qslot:512 * (qslot + 1)]
                    qto = HT[0:64, 512 * qslot:512 * (qslot + 1)]
                    hx = RT[64:128, :] if qslot == 0 else HT[64:128, :]
                    if qslot_dve[qslot]:
                        wait("scalar", sDVE, qslot_dve[qslot])
                    ew("scalar", lambda en, m=m, bset=bset, qte=qte: en.activation(
                        qte, PS[bset[m]][0:64, :], AF.Copy))
                    if m < 2:
                        ew("scalar", lambda en, m=m, bset=bset, qto=qto:
                           en.activation(qto, PS[bset[m]][64:128, :], AF.Copy))
                    else:
                        ew("scalar", lambda en, n=n, m=m, bset=bset: en.activation(
                            KV[64:128, 512 * n:512 * (n + 1)],
                            PS[bset[m]][64:128, :], AF.Copy))
                    fence("scalar", inc=sACT)
                    rel_bank(bset[m], sACT)
                    wait("vector", sACT)
                    tqc = RB[0:64, 512 * n:512 * (n + 1)]
                    tqs = RB[0:64, 8192 + 512 * n:8192 + 512 * (n + 1)]
                    if m < 2:
                        if hx_dma[qslot]:
                            wait("vector", sS, hx_dma[qslot])
                        if qsq_act[qslot]:
                            wait("vector", sACT, qsq_act[qslot])
                        ew("vector", lambda en, qte=qte, tqc=tqc: en.tensor_mul(
                            RT[0:64, 0:512], qte, tqc))
                        ew("vector", lambda en, qto=qto, tqs=tqs: en.tensor_mul(
                            RT[0:64, 512:1024], qto, tqs))
                        ew("vector", lambda en, hx=hx: en.tensor_sub(
                            hx[:, 0:512], RT[0:64, 0:512], RT[0:64, 512:1024]))
                        ew("vector", lambda en, qte=qte, tqs=tqs: en.tensor_mul(
                            RT[0:64, 0:512], qte, tqs))
                        ew("vector", lambda en, qto=qto, tqc=tqc: en.tensor_mul(
                            RT[0:64, 512:1024], qto, tqc))
                        ew("vector", lambda en, hx=hx: en.tensor_add(
                            hx[:, 512:1024], RT[0:64, 0:512], RT[0:64, 512:1024]))
                        fence("vector", inc=sDVE)
                        qslot_dve[qslot] = sDVE.n
                        wait("scalar", sDVE, qslot_dve[qslot])
                        ew("scalar", lambda en, hx=hx: en.activation(
                            QT[64:128, 0:1024], hx[:, :], AF.Square))
                        fence("scalar", inc=sACT)
                        qsq_act[qslot] = sACT.n
                        pe_bank(7)
                        wait("tensor", sACT)
                        ew("tensor", lambda en: en.matmul(
                            PS[7][0:1, :], ONESC[64:128, :], QT[64:128, 0:512],
                            start=True, stop=False))
                        ew("tensor", lambda en: en.matmul(
                            PS[7][0:1, :], ONESC[64:128, :], QT[64:128, 512:1024],
                            start=False, stop=True), inc=sPE)
                        wait("vector", sPE)
                        ew("vector", lambda en: en.reduce_max(
                            QA[0:1, 1:2], PS[7][0:1, :], axis=AX.X))
                        fence("vector")
                        if not qa_q:
                            ew("vector", lambda en: en.tensor_copy(
                                QA[0:1, 0:1], QA[0:1, 1:2]))
                            qa_q = True
                        else:
                            ew("vector", lambda en: en.tensor_max(
                                QA[0:1, 0:1], QA[0:1, 0:1], QA[0:1, 1:2]))
                        fence("vector", inc=sDVE)
                        rel_bank(7, sDVE)
                        dma("scalar",
                            qp_dram[128 * m:128 * (m + 1), 512 * n:512 * (n + 1)]
                            .rearrange("(h p) t -> p h t", h=2),
                            hx[:, :].rearrange("p (h t) -> p h t", h=2))
                        hx_dma[qslot] = sS.n
                    else:
                        kvd = KV[:, 512 * n:512 * (n + 1)]
                        ew("vector", lambda en, qte=qte, tqc=tqc: en.tensor_mul(
                            RT[0:32, 0:512], qte[0:32, :], tqc[0:32, :]))
                        ew("vector", lambda en, qte=qte, tqs=tqs: en.tensor_mul(
                            RT[0:32, 512:1024], qte[32:64, :], tqs[32:64, :]))
                        ew("vector", lambda en, kvd=kvd: en.tensor_sub(
                            kvd[0:32, :], RT[0:32, 0:512], RT[0:32, 512:1024]))
                        ew("vector", lambda en, qte=qte, tqs=tqs: en.tensor_mul(
                            RT[0:32, 0:512], qte[0:32, :], tqs[0:32, :]))
                        ew("vector", lambda en, qte=qte, tqc=tqc: en.tensor_mul(
                            RT[0:32, 512:1024], qte[32:64, :], tqc[32:64, :]))
                        ew("vector", lambda en, kvd=kvd: en.tensor_add(
                            kvd[32:64, :], RT[0:32, 0:512], RT[0:32, 512:1024]))
                        fence("vector", inc=sDVE)
                        qslot_dve[qslot] = sDVE.n
                        wait("scalar", sDVE, qslot_dve[qslot])
                        ew("scalar", lambda en, kvd=kvd: en.activation(
                            QT[64:128, 0:512], kvd[0:64, :], AF.Square))
                        fence("scalar", inc=sACT)
                        qsq_act[qslot] = sACT.n
                        pe_bank(7)
                        wait("tensor", sACT)
                        ew("tensor", lambda en: en.matmul(
                            PS[7][0:1, :], ONESC[64:128, :], QT[64:128, 0:512],
                            start=True, stop=True), inc=sPE)
                        wait("vector", sPE)
                        ew("vector", lambda en: en.reduce_max(
                            QA[0:1, 3:4], PS[7][0:1, :], axis=AX.X))
                        fence("vector")
                        if not qa_k:
                            ew("vector", lambda en: en.tensor_copy(
                                QA[0:1, 2:3], QA[0:1, 3:4]))
                            qa_k = True
                        else:
                            ew("vector", lambda en: en.tensor_max(
                                QA[0:1, 2:3], QA[0:1, 2:3], QA[0:1, 3:4]))
                        fence("vector", inc=sDVE)
                        rel_bank(7, sDVE)
            rope_all_done = sDVE.n
            ew("vector", lambda en: en.tensor_mul(UROW[:, 202:203],
                                                  QA[0:1, 0:1], QA[0:1, 2:3]))
            fence("vector", inc=sDVE)
            wait("scalar", sDVE)
            ew("scalar", lambda en: en.activation(UROW[:, 202:203],
                                                  UROW[:, 202:203], AF.Ln))
            fence("scalar")
            ew("scalar", lambda en: en.activation(UROW[:, 202:203],
                                                  UROW[:, 202:203], AF.Exp,
                                                  scale=0.5))
            fence("scalar", inc=sACT)
            wait("vector", sACT)
            ew("vector", lambda en: en.tensor_single_scalar(
                UROW[:, 203:204], UROW[:, 202:203], -0.125, ALU.mult))
            fence("vector", inc=sDVE)
            u_row_done = sDVE.n
            pe_bank(6)
            wait("tensor", sDVE, u_row_done)
            ew("tensor", lambda en: en.matmul(PS[6][:, 0:1], ONES1[:, :],
                                              UROW[:, 203:204],
                                              start=True, stop=True), inc=sPE)
            wait("vector", sPE)
            ew("vector", lambda en: en.tensor_copy(NEGUB[:, :], PS[6][:, 0:1]))
            fence("vector", inc=sDVE)
            rel_bank(6, sDVE)

        def phase_attn():
            kv_done = sDVE.n
            qp_stores = sS.n
            ew("vector", lambda en: en.memset(VAUG[:, :], 1.0), inc=sDVE)
            vaug_ones = sDVE.n
            for kt in range(64):
                b = 4 + kt % 2
                pe_bank(b)
                if kt == 0:
                    wait("tensor", sDVE, kv_done)
                ew("tensor", lambda en, kt=kt, b=b: en.transpose(
                    PS[b].bitcast(BF16)[:, 0:64],
                    KV[64:128, 128 * kt:128 * (kt + 1)],
                    IDBF[64:128, 64:128]), inc=sPE)
                wait("vector", sPE)
                if kt == 0:
                    wait("vector", sDVE, vaug_ones)
                ew("vector", lambda en, kt=kt, b=b: en.tensor_copy(
                    VAUG[:, 65 * kt:65 * kt + 64],
                    PS[b].bitcast(BF16)[:, 0:64]), inc=sDVE)
                rel_bank(b, sDVE)
            vaug_done = sDVE.n

            qext = RE[0:64, :]
            pt_slots = [QT[:, 0:1024], HT[:, 0:1024]]
            wait("scalar", sS, qp_stores)
            wait("vector", sS, qp_stores)
            y_store = []
            for h in range(HLOC):
                p = h // 2
                isB = h % 2 == 1
                wait("sync", sPE)
                if h == 0:
                    wait("sync", sS, qp_stores)
                re_ = 128 * p + (32 if isB else 0)
                ro_ = 128 * p + 64 + (32 if isB else 0)
                dma("sync", RE[0:32, :], qp_dram[re_:re_ + 32, :])
                dma("sync", RE[32:64, :], qp_dram[ro_:ro_ + 32, :])
                q_ready = sX.n
                wait("tensor", sX, q_ready)
                if h == 0:
                    wait("tensor", sDVE, vaug_done)
                praw_dve = 0
                for qc in range(16):
                    accb = 6 + qc % 2
                    ktmax = 4 * qc + 4
                    pe_bank(accb)
                    av_pe = []
                    s_pe = []

                    def emit_s(kt, qc=qc):
                        sb = kt % 2
                        pe_bank(sb)
                        ew("tensor", lambda en, kt=kt, qc=qc, sb=sb: en.matmul(
                            PS[sb], KV[0:64, 128 * kt:128 * (kt + 1)],
                            qext[:, 512 * qc:512 * (qc + 1)],
                            start=True, stop=True), inc=sPE)
                        s_pe.append(sPE.n)

                    emit_s(0)
                    for kt in range(ktmax):
                        sb = kt % 2
                        diag = kt >= 4 * qc
                        wait("scalar", sPE, s_pe[kt])
                        if kt >= 2:
                            wait("scalar", sPE, av_pe[kt - 2])
                        if diag and praw_dve:
                            wait("scalar", sDVE, praw_dve)
                        dst = HT[:, 0:512] if diag else QT[:, 512 * sb:512 * (sb + 1)]
                        ew("scalar", lambda en, sb=sb, dst=dst: en.activation(
                            dst, PS[sb], AF.Exp, bias=NEGUB[:, 0:1],
                            scale=0.125), inc=sACT)
                        rel_bank(sb, sACT)
                        if diag:
                            r = kt - 4 * qc
                            wait("vector", sACT)
                            if kt >= 2:
                                wait("vector", sPE, av_pe[kt - 2])
                            ew("vector", lambda en, sb=sb, r=r: en.tensor_mul(
                                QT[:, 512 * sb:512 * (sb + 1)], HT[:, 0:512],
                                M01S[:, 512 * r:512 * (r + 1)]), inc=sDVE)
                            praw_dve = sDVE.n
                        if kt + 1 < ktmax:
                            emit_s(kt + 1)
                        if diag:
                            wait("tensor", sDVE, praw_dve)
                        else:
                            wait("tensor", sACT)
                        ew("tensor", lambda en, kt=kt, sb=sb, accb=accb, ktmax=ktmax:
                           en.matmul(
                               PS[accb][0:65, :], VAUG[:, 65 * kt:65 * kt + 65],
                               QT[:, 512 * sb:512 * (sb + 1)],
                               start=(kt == 0), stop=(kt == ktmax - 1),
                               skip_group_check=True), inc=sPE)
                        av_pe.append(sPE.n)
                    wait("scalar", sPE, av_pe[-1])
                    ew("scalar", lambda en, accb=accb: en.activation(
                        ACCS[0:65, :], PS[accb][0:65, :], AF.Copy))
                    fence("scalar", inc=sACT)
                    rel_bank(accb, sACT)
                    wait("vector", sACT)
                    ew("vector", lambda en: en.reciprocal(REC, ACCS[64:65, :]))
                    fence("vector", inc=sDVE)
                    pe_bank(2)
                    wait("tensor", sDVE)
                    ew("tensor", lambda en: en.matmul(PS[2][0:64, :],
                                                      ONES1[0:1, 0:64], REC,
                                                      start=True, stop=True), inc=sPE)
                    yslot = qc % 2
                    if len(y_store) >= 2:
                        wait("vector", sS, y_store[-2])
                    wait("vector", sPE)
                    ew("vector", lambda en, yslot=yslot: en.tensor_mul(
                        YO[:, 512 * yslot:512 * (yslot + 1)], ACCS[0:64, :],
                        PS[2][0:64, :]), inc=sDVE)
                    rel_bank(2, sDVE)
                    wait("scalar", sDVE)
                    dma("scalar", cc2_in[64 * h:64 * (h + 1), 512 * qc:512 * (qc + 1)],
                        YO[:, 512 * yslot:512 * (yslot + 1)])
                    y_store.append(sS.n)
            wait("gpsimd", sS, y_store[-1])
            ew("gpsimd", lambda en: en.collective_compute(
                "AllGather", ALU.bypass, replica_groups=[list(range(NCORES))],
                ins=[cc2_in], outs=[cc2_out]), inc=sCC)

        def phase_wo():
            ag2_done = sCC.n
            wait("gpsimd", sCC, ag2_done)
            wait("gpsimd", sPE)
            src = cc2_out.rearrange("f (b t) -> (f b) t", b=NCORES)
            gdone = []
            for k in range(NCH):
                ew("gpsimd", lambda en, k=k: en.indirect_dma_start(
                    out=RA[:, 1024 * k:1024 * (k + 1)],
                    out_offset=None,
                    in_=src,
                    in_offset=IndirectOffsetOnAxis(WOIDX[:, k:k + 1], 0)),
                   inc=sW, k=16)
                gdone.append(sW.n)
            wo_ready = {}
            for mg in range(2):
                slot = mg % 2
                dma("gpsimd",
                    RB[:, 4096 * slot:4096 * (slot + 1)].rearrange(
                        "p (k m) -> p k m", m=256),
                    woT_d[:, 256 * mg:256 * (mg + 1)].rearrange(
                        "(k p) m -> p k m", p=128), sc=slot_sems[slot])
                wo_ready[mg] = slot_sems[slot].n
            for mg in range(8):
                slot = mg % 2
                wos = RB[:, 4096 * slot:4096 * (slot + 1)]
                wait("tensor", slot_sems[slot], wo_ready[mg])
                wait("tensor", sW, gdone[-1])
                for half in range(2):
                    bi = (0, 1) if (2 * mg + half) % 2 == 0 else (2, 3)
                    for b in bi:
                        pe_bank(b)
                    for k in range(NCH):
                        for m in range(2):
                            ew("tensor", lambda en, k=k, m=m, half=half, wos=wos,
                               bi=bi: en.matmul(
                                   PS[bi[m]][:, :],
                                   wos[:, 256 * k + 128 * m:256 * k + 128 * (m + 1)],
                                   RA[:, 1024 * k + 512 * half:
                                      1024 * k + 512 * (half + 1)],
                                   start=(k == 0), stop=(k == NCH - 1)),
                               inc=(sPE if (k == NCH - 1 and m == 1) else None))
                    wait("vector", sPE)
                    for m in range(2):
                        dc = 2 * mg + m
                        xc = X[:, 1024 * dc + 512 * half:1024 * dc + 512 * (half + 1)]
                        ew("vector", lambda en, bi=bi, m=m, xc=xc: en.tensor_add(
                            xc, xc, PS[bi[m]][:, :]), inc=sDVE)
                    for b in bi:
                        rel_bank(b, sDVE)
                if mg + 2 < 8:
                    wait("gpsimd", sPE)
                    dma("gpsimd",
                        RB[:, 4096 * slot:4096 * (slot + 1)].rearrange(
                            "p (k m) -> p k m", m=256),
                        woT_d[:, 256 * (mg + 2):256 * (mg + 3)].rearrange(
                            "(k p) m -> p k m", p=128), sc=slot_sems[slot])
                    wo_ready[mg + 2] = slot_sems[slot].n
            slab_guard[("w13", 0)] = sPE.n
            slab_guard[("w13", 1)] = sPE.n
            slab_guard[("w2", 0)] = sPE.n
            slab_guard[("w2", 1)] = sPE.n

        stop = PHASES.index(upto)
        phase_fns = {"ln1": phase_ln, "ffn1": phase_ffn, "ln2": phase_ln,
                     "qkv": phase_qkv, "attn": phase_attn, "wo": phase_wo,
                     "ln3": phase_ln, "ffn2": phase_ffn}
        for i, name in enumerate(PHASES):
            if i > stop:
                break
            phase_fns[name]()

        if "xln" in tap_aps or "tap_xln" in tap_aps:
            pass
        if "tap_xln" in tap_aps:
            wait("sync", sDVE)
            dma("sync", tap_aps["tap_xln"], RA[:, :])
        if "tap_re" in tap_aps:
            wait("sync", sDVE)
            dma("sync", tap_aps["tap_re"], RE[:, :])
        if "tap_mfm" in tap_aps:
            wait("sync", sDVE)
            dma("sync", tap_aps["tap_mfm"], MFM[:, :])
        if "tap_negub" in tap_aps:
            wait("sync", sDVE)
            dma("sync", tap_aps["tap_negub"], NEGUB[:, :])
            dma("sync", tap_aps["tap_urow"], ROWS[0:1, 512:1024])
        if "tap_mbb" in tap_aps:
            wait("sync", sACT)
            dma("sync", tap_aps["tap_mbb"], MBB[:, :])
        if "tap_kv" in tap_aps:
            wait("sync", sDVE)
            dma("sync", tap_aps["tap_kv"], KV[:, :])
        if "tap_vaug" in tap_aps:
            wait("sync", sDVE)
            dma("sync", tap_aps["tap_vaug"], VAUG[:, :])
        if "tap_qp" in tap_aps:
            wait("sync", sS)
            wait("sync", sDVE)
            for half in range(4):
                dma("sync", RA[:, 0:4096], qp_dram[:, :].rearrange(
                    "(a p) t -> p a t", p=128)[:, :, 2048 * half:2048 * (half + 1)])
                wait("sync", sX)
                dma("sync", tap_aps["tap_qp"].rearrange("(a p) t -> p a t", p=128)
                    [:, :, 2048 * half:2048 * (half + 1)], RA[:, 0:4096])
                wait("sync", sX)
        if "tap_cc2" in tap_aps:
            wait("sync", sCC)
            for half in range(4):
                dma("sync", RA[:, 0:4096], cc2_in.rearrange(
                    "(a p) t -> p a t", p=128)[:, :, 2048 * half:2048 * (half + 1)])
                wait("sync", sX)
                dma("sync", tap_aps["tap_cc2"].rearrange("(a p) t -> p a t", p=128)
                    [:, :, 2048 * half:2048 * (half + 1)], RA[:, 0:4096])
                wait("sync", sX)

        wait("sync", sDVE)
        wait("sync", sPE)
        dma("sync", out_x.rearrange("(c p) t -> p c t", p=128),
            X[:, :].rearrange("p (c t) -> p c t", t=TL))
        wait("sync", sX)

        @block.sync
        def _(en):
            for f in prog["sync"]:
                f(en)

        @block.scalar
        def _(en):
            for f in prog["scalar"]:
                f(en)

        @block.vector
        def _(en):
            for f in prog["vector"]:
                f(en)

        @block.tensor
        def _(en):
            for f in prog["tensor"]:
                f(en)

        @block.gpsimd
        def _(en):
            for f in prog["gpsimd"]:
                f(en)

    return nc



def host_prep(inputs):
    x = np.asarray(inputs["L0_attn_x"], dtype=np.float32).reshape(T, D)
    wq = np.asarray(inputs["L0_attn_wq"], dtype=np.float32)
    wk = np.asarray(inputs["L0_attn_wk"], dtype=np.float32)
    wv = np.asarray(inputs["L0_attn_wv"], dtype=np.float32)
    wo = np.asarray(inputs["L0_attn_wo"], dtype=np.float32)
    eb = np.asarray(inputs["L0_ffn_expert_bias"], dtype=np.float32)
    gw = np.asarray(inputs["L0_ffn_gate_w"], dtype=np.float32)
    w1 = np.asarray(inputs["L0_ffn_w1"], dtype=np.float32)
    w2 = np.asarray(inputs["L0_ffn_w2"], dtype=np.float32)
    w3 = np.asarray(inputs["L0_ffn_w3"], dtype=np.float32)
    sw1 = np.asarray(inputs["L0_ffn_sw1"], dtype=np.float32)
    sw2 = np.asarray(inputs["L0_ffn_sw2"], dtype=np.float32)
    sw3 = np.asarray(inputs["L0_ffn_sw3"], dtype=np.float32)

    C = np.ascontiguousarray

    w1T = np.stack([C(w1[i].T) for i in range(8)] + [C(sw1.T)])
    w3T = np.stack([C(w3[i].T) for i in range(8)] + [C(sw3.T)])
    w2T = np.stack([C(w2[i].T) for i in range(8)] + [C(sw2.T)])
    gateT = C(gw.T)
    biasT = np.tile(eb[None, :], (128, 1)).astype(np.float32)
    woT = C(wo.T)

    theta = 1.0 / (10000.0 ** (np.arange(0, HD, 2, dtype=np.float32) / HD))
    ang = np.arange(T, dtype=np.float32)[:, None] * theta[None, :]
    cosT = np.cos(ang).T.astype(np.float32)
    sinT = np.sin(ang).T.astype(np.float32)
    cos2 = np.concatenate([cosT, cosT], axis=0)
    sin2 = np.concatenate([sinT, sinT], axis=0)
    tq = np.concatenate([cos2, sin2], axis=1).astype(BF)

    kk = np.arange(128)[:, None]
    qq = np.arange(512)[None, :]
    m01 = np.concatenate([(kk <= qq - 128 * r) for r in range(4)],
                         axis=1).astype(BF)

    idbf = np.eye(128, dtype=BF)
    idf32 = np.eye(128, dtype=np.float32)
    ones1 = np.ones((1, 128), dtype=np.float32)
    onescol = np.ones((128, 1), dtype=BF)
    oneh = np.zeros((8, 1024), dtype=BF)
    for ex in range(8):
        oneh[ex, 128 * ex:128 * (ex + 1)] = 1.0

    per_core = []
    for c in range(NCORES):
        xs = C(x[TL * c:TL * (c + 1)].T)
        rows = []
        for pair in range(2):
            hA = 4 * c + 2 * pair
            hB = hA + 1
            qa = wq[HD * hA:HD * (hA + 1)]
            qb = wq[HD * hB:HD * (hB + 1)]
            rows += [qa[0::2], qb[0::2], qa[1::2], qb[1::2]]
        kc = wk[HD * c:HD * (c + 1)]
        rows += [kc[0::2], kc[1::2], wv[HD * c:HD * (c + 1)]]
        wqkvT = C(np.concatenate(rows, axis=0).T)
        woidx = ((128 * np.arange(NCH)[None, :] + np.arange(128)[:, None]) * NCORES
                 + c).astype(np.int32)
        per_core.append({
            "x_in": xs, "wqkvT": wqkvT, "woT": woT,
            "w1T": w1T, "w3T": w3T, "w2T": w2T,
            "gateT": gateT, "biasT": biasT,
            "tq_in": tq, "m01_in": m01,
            "idbf_in": idbf, "idf32_in": idf32, "ones_in": ones1,
            "onescol_in": onescol, "oneh_in": oneh, "woidx_in": woidx,
        })
    return per_core


def kernel(**inputs):
    per_core = host_prep(inputs)
    nc = build_nc()
    res = run_bass_kernel_spmd(nc, per_core, core_ids=list(range(NCORES)))
    outs = [res.results[c]["out_x"] for c in range(NCORES)]
    full = np.concatenate([o.T for o in outs], axis=0)
    return full.reshape(1, T, D).astype(np.float32)


# version 06
# speedup vs baseline: 1.0165x; 1.0165x over previous
import numpy as np
import ml_dtypes
from contextlib import ExitStack

import concourse.bass as bass
import concourse.mybir as mybir
from concourse.bass_utils import run_bass_kernel_spmd
from concourse.bass_core_rust import IndirectOffsetOnAxis

F32 = mybir.dt.float32
BF16 = mybir.dt.bfloat16
I32 = mybir.dt.int32
AF = mybir.ActivationFunctionType
ALU = mybir.AluOpType
AX = mybir.AxisListType
BF = ml_dtypes.bfloat16

D = 2048
T = 8192
NCORES = 8
TL = T // NCORES
NCH = D // 128
HID = 1024
NEXP = 9
HLOC = 4
HD = 64
EPS = 1e-5

PHASES = ("ln1", "ffn1", "ln2", "qkv", "attn", "wo", "ln3", "ffn2")


class Ctr:
    def __init__(self, handle):
        self.h = handle
        self.n = 0


def build_nc(upto="ffn2", taps=(), ffn_mode="full"):
    nc = bass.Bass("TRN2", target_bir_lowering=False)

    x_in = nc.dram_tensor("x_in", [D, TL], F32, kind="ExternalInput").ap()
    wqkvT_d = nc.dram_tensor("wqkvT", [D, 384], F32, kind="ExternalInput").ap()
    woT_d = nc.dram_tensor("woT", [D, D], F32, kind="ExternalInput").ap()
    w1T_d = nc.dram_tensor("w1T", [NEXP, D, HID], F32, kind="ExternalInput").ap()
    w3T_d = nc.dram_tensor("w3T", [NEXP, D, HID], F32, kind="ExternalInput").ap()
    w2T_d = nc.dram_tensor("w2T", [NEXP, HID, D], F32, kind="ExternalInput").ap()
    gateT_d = nc.dram_tensor("gateT", [D, 8], F32, kind="ExternalInput").ap()
    biasT_d = nc.dram_tensor("biasT", [128, 8], F32, kind="ExternalInput").ap()
    tq_in = nc.dram_tensor("tq_in", [64, 2 * T], BF16, kind="ExternalInput").ap()
    m01_in = nc.dram_tensor("m01_in", [128, 2048], BF16, kind="ExternalInput").ap()
    idbf_in = nc.dram_tensor("idbf_in", [128, 128], BF16, kind="ExternalInput").ap()
    idf32_in = nc.dram_tensor("idf32_in", [128, 128], F32, kind="ExternalInput").ap()
    ones_in = nc.dram_tensor("ones_in", [1, 128], F32, kind="ExternalInput").ap()
    onescol_in = nc.dram_tensor("onescol_in", [128, 1], BF16, kind="ExternalInput").ap()
    oneh_in = nc.dram_tensor("oneh_in", [8, 1024], BF16, kind="ExternalInput").ap()
    woidx_in = nc.dram_tensor("woidx_in", [128, 16], I32, kind="ExternalInput").ap()

    out_x = nc.dram_tensor("out_x", [D, TL], F32, kind="ExternalOutput").ap()
    tap_aps = {}
    if "xln" in taps:
        tap_aps["tap_xln"] = nc.dram_tensor("tap_xln", [128, 16384], BF16,
                                            kind="ExternalOutput").ap()
    if "qp" in taps:
        tap_aps["tap_qp"] = nc.dram_tensor("tap_qp", [256, T], BF16,
                                           kind="ExternalOutput").ap()
    if "kv" in taps:
        tap_aps["tap_kv"] = nc.dram_tensor("tap_kv", [128, T], BF16,
                                           kind="ExternalOutput").ap()
    if "cc2" in taps:
        tap_aps["tap_cc2"] = nc.dram_tensor("tap_cc2", [256, T], BF16,
                                            kind="ExternalOutput").ap()
    if "negub" in taps:
        tap_aps["tap_negub"] = nc.dram_tensor("tap_negub", [128, 1], F32,
                                              kind="ExternalOutput").ap()
        tap_aps["tap_urow"] = nc.dram_tensor("tap_urow", [1, 512], F32,
                                             kind="ExternalOutput").ap()
    if "re" in taps:
        tap_aps["tap_re"] = nc.dram_tensor("tap_re", [128, 8192], BF16,
                                           kind="ExternalOutput").ap()
    if "mfm" in taps:
        tap_aps["tap_mfm"] = nc.dram_tensor("tap_mfm", [8, 1024], BF16,
                                            kind="ExternalOutput").ap()
    if "mbb" in taps:
        tap_aps["tap_mbb"] = nc.dram_tensor("tap_mbb", [128, 1024], BF16,
                                            kind="ExternalOutput").ap()
    if "vaug" in taps:
        tap_aps["tap_vaug"] = nc.dram_tensor("tap_vaug", [128, 65 * 64], BF16,
                                             kind="ExternalOutput").ap()

    cc1_in = nc.dram_tensor("cc1_in", [D, TL], BF16).ap()
    cc1_out = nc.dram_tensor("cc1_out", [NCORES * D, TL], BF16).ap()
    cc2_in = nc.dram_tensor("cc2_in", [256, T], BF16).ap()
    cc2_out = nc.dram_tensor("cc2_out", [D, T], BF16).ap()
    qp_dram = nc.dram_tensor("qp_dram", [256, T], BF16).ap()

    with ExitStack() as st:
        e = st.enter_context
        X = e(nc.sbuf_tensor("X", [128, NCH * TL], F32))
        RA = e(nc.sbuf_tensor("RA", [128, 16384], BF16))
        RB = e(nc.sbuf_tensor("RB", [128, 20480], BF16))
        KV = e(nc.sbuf_tensor("KV", [128, T], BF16))
        RE = e(nc.sbuf_tensor("RE", [128, 8192], BF16))
        VAUG = e(nc.sbuf_tensor("VAUG", [128, 65 * 64], BF16))
        M01S = e(nc.sbuf_tensor("M01S", [128, 2048], BF16))
        ROWS = e(nc.sbuf_tensor("ROWS", [1, 1536], F32))
        MFM = e(nc.sbuf_tensor("MFM", [8, 1024], BF16))
        SEL = e(nc.sbuf_tensor("SEL", [128, 80], F32))
        MBB = e(nc.sbuf_tensor("MBB", [128, 1024], BF16))
        HT = e(nc.sbuf_tensor("HT", [128, 1024], BF16))
        QT = e(nc.sbuf_tensor("QT", [128, 1024], BF16))
        RT = e(nc.sbuf_tensor("RT", [128, 1024], BF16))
        YO = e(nc.sbuf_tensor("YO", [64, 1024], BF16))
        ACCS = e(nc.sbuf_tensor("ACCS", [128, 512], F32))
        QA = e(nc.sbuf_tensor("QA", [128, 4], F32))
        NEGUB = e(nc.sbuf_tensor("NEGUB", [128, 1], F32))
        IDBF = e(nc.sbuf_tensor("IDBF", [128, 128], BF16))
        IDF32 = e(nc.sbuf_tensor("IDF32", [128, 128], F32))
        ONES1 = e(nc.sbuf_tensor("ONES1", [1, 128], F32))
        ONESC = e(nc.sbuf_tensor("ONESC", [128, 1], BF16))
        GATET = e(nc.sbuf_tensor("GATET", [128, 128], BF16))
        BIASB = e(nc.sbuf_tensor("BIASB", [128, 8], F32))
        ONEH = e(nc.sbuf_tensor("ONEH", [8, 1024], BF16))
        WOIDX = e(nc.sbuf_tensor("WOIDX", [128, 16], I32))

        REC = ROWS[0:1, 0:512]
        LNT = None
        UROW = ROWS[0:1, 512:1024]
        PSW = [e(nc.psum_tensor(f"psw{i}", [128, 1024], F32)) for i in range(4)]
        PS = [PSW[i // 2][:, 512 * (i % 2):512 * (i % 2 + 1)] for i in range(8)]

        sX = Ctr(e(nc.semaphore("sX")))
        sS = Ctr(e(nc.semaphore("sS")))
        sW = Ctr(e(nc.semaphore("sW")))
        sA0 = Ctr(e(nc.semaphore("sA0")))
        sA1 = Ctr(e(nc.semaphore("sA1")))
        sPE = Ctr(e(nc.semaphore("sPE")))
        sDVE = Ctr(e(nc.semaphore("sDVE")))
        sACT = Ctr(e(nc.semaphore("sACT")))
        sCC = Ctr(e(nc.semaphore("sCC")))
        block = e(nc.Block())

        prog = {k: [] for k in ("sync", "scalar", "vector", "tensor", "gpsimd")}

        def ew(eng, fn, inc=None, k=1):
            if inc is not None:
                inc.n += k
                def f(en, fn=fn, h=inc.h, k=k):
                    fn(en).then_inc(h, k)
            else:
                def f(en, fn=fn):
                    fn(en)
            prog[eng].append(f)

        def wait(eng, sc, n=None):
            tgt = sc.n if n is None else n
            prog[eng].append(lambda en, h=sc.h, v=tgt: en.wait_ge(h, v))

        def dma(eng, out, in_, sc=None, **kw):
            sc = sc if sc is not None else {"sync": sX, "scalar": sS, "gpsimd": sW}[eng]
            ew(eng, lambda en, o=out, i=in_, kw=kw: en.dma_start(out=o, in_=i, **kw),
               inc=sc, k=16)
            return sc.n

        def fence(eng, inc=None, k=1):
            ew(eng, lambda en: en.drain(), inc=inc, k=k)

        bank_rel = [None] * 8

        def pe_bank(b):
            r = bank_rel[b]
            if r is not None:
                wait("tensor", r[0], r[1])
                bank_rel[b] = None

        def rel_bank(b, sc):
            bank_rel[b] = (sc, sc.n)

        dma("sync", X[:, :].rearrange("p (c t) -> p c t", t=TL),
            x_in.rearrange("(c p) t -> p c t", p=128))
        dma("sync", IDBF[:, :], idbf_in)
        dma("sync", IDF32[:, :], idf32_in)
        dma("sync", ONES1[:, :], ones_in)
        dma("sync", ONESC[:, :], onescol_in)
        dma("sync", BIASB[:, :], biasT_d)
        dma("sync", ONEH[:, :], oneh_in)
        dma("sync", WOIDX[:, :], woidx_in)
        dma("sync", M01S[:, :], m01_in)
        init_sync = sX.n
        dma("gpsimd", GATET[:, :].rearrange("p (k e) -> p k e", e=8),
            gateT_d.rearrange("(k p) e -> p k e", p=128))
        init_gp = sW.n

        def phase_ln():
            wait("vector", sX, init_sync)
            wait("scalar", sX, init_sync)
            wait("scalar", sDVE)
            for b in range(4):
                pe_bank(b)
            pe_after_mm = []
            cslots = [HT, RT]
            qslots = [QT, MBB]
            for c in range(NCH):
                sl = c % 2
                if c >= 2:
                    wait("vector", sPE, pe_after_mm[c - 2])
                    wait("scalar", sPE, pe_after_mm[c - 2])
                ew("vector", lambda en, c=c, sl=sl: en.tensor_copy(
                    cslots[sl][:, :], X[:, 1024 * c:1024 * (c + 1)]), inc=sDVE)
                ew("scalar", lambda en, c=c, sl=sl: en.activation(
                    qslots[sl][:, :], X[:, 1024 * c:1024 * (c + 1)], AF.Square),
                   inc=sACT)
                wait("tensor", sDVE)
                wait("tensor", sACT)
                for h in range(2):
                    ew("tensor", lambda en, c=c, sl=sl, h=h: en.matmul(
                        PS[h][0:1, :], ONESC[:, :],
                        cslots[sl][:, 512 * h:512 * (h + 1)],
                        start=(c == 0), stop=(c == NCH - 1)))
                    ew("tensor", lambda en, c=c, sl=sl, h=h: en.matmul(
                        PS[2 + h][0:1, :], ONESC[:, :],
                        qslots[sl][:, 512 * h:512 * (h + 1)],
                        start=(c == 0), stop=(c == NCH - 1)), inc=sPE)
                pe_after_mm.append(sPE.n)
            pe_stats_done = sPE.n
            r_rstd = ROWS[0:1, 0:512]
            r_mr = ROWS[0:1, 512:1024]
            r_scr = ROWS[0:1, 1024:1536]
            for h in range(2):
                wait("vector", sPE, pe_stats_done)
                ew("vector", lambda en, h=h: en.tensor_scalar_mul(
                    r_scr, PS[h][0:1, :], 1.0 / D))
                fence("vector")
                ew("vector", lambda en: en.tensor_mul(r_mr, r_scr, r_scr))
                ew("vector", lambda en, h=h: en.tensor_scalar_mul(
                    r_rstd, PS[2 + h][0:1, :], 1.0 / D))
                fence("vector")
                ew("vector", lambda en: en.tensor_sub(r_rstd, r_rstd, r_mr))
                fence("vector")
                ew("vector", lambda en: en.tensor_scalar_add(r_rstd, r_rstd, EPS))
                fence("vector", inc=sDVE)
                wait("scalar", sDVE)
                ew("scalar", lambda en: en.activation(r_rstd, r_rstd, AF.Ln))
                fence("scalar")
                ew("scalar", lambda en: en.activation(r_rstd, r_rstd, AF.Exp,
                                                      scale=-0.5))
                fence("scalar", inc=sACT)
                wait("vector", sACT)
                ew("vector", lambda en: en.tensor_mul(r_mr, r_scr, r_rstd))
                fence("vector", inc=sDVE)
                if h == 1:
                    for b in range(4):
                        rel_bank(b, sDVE)
                pe_bank(4)
                pe_bank(5)
                wait("tensor", sDVE)
                ew("tensor", lambda en: en.matmul(PS[4][:, :], ONES1[:, :], r_rstd,
                                                  start=True, stop=True))
                ew("tensor", lambda en: en.matmul(PS[5][:, :], ONES1[:, :], r_mr,
                                                  start=True, stop=True), inc=sPE)
                wait("vector", sPE)
                for c in range(NCH):
                    ew("vector", lambda en, c=c, h=h: en.tensor_mul(
                        ACCS[:, :], X[:, 1024 * c + 512 * h:1024 * c + 512 * (h + 1)],
                        PS[4][:, :]))
                    ew("vector", lambda en, c=c, h=h: en.tensor_sub(
                        RA[:, 1024 * c + 512 * h:1024 * c + 512 * (h + 1)],
                        ACCS[:, :], PS[5][:, :]), inc=sDVE)
                rel_bank(4, sDVE)
                rel_bank(5, sDVE)

        slot_sems = [sA0, sA1]
        slab_guard = {}

        def load_slab13(ex, mg):
            slot = mg % 2
            sc = slot_sems[slot]
            g = slab_guard.pop(("w13", slot), None)
            if g is not None:
                wait("gpsimd", sPE, g)
            dma("gpsimd",
                RB[:, 4096 * slot:4096 * (slot + 1)].rearrange("p (k m) -> p k m",
                                                               m=256),
                w1T_d[ex, :, 256 * mg:256 * (mg + 1)].rearrange("(k p) m -> p k m",
                                                                p=128), sc=sc)
            dma("gpsimd",
                RB[:, 8192 + 4096 * slot:8192 + 4096 * (slot + 1)].rearrange(
                    "p (k m) -> p k m", m=256),
                w3T_d[ex, :, 256 * mg:256 * (mg + 1)].rearrange("(k p) m -> p k m",
                                                               p=128), sc=sc)
            return sc.n

        def load_slab2(ex, dg):
            slot = dg % 2
            sc = slot_sems[slot]
            g = slab_guard.pop(("w2", slot), None)
            if g is not None:
                wait("gpsimd", sPE, g)
            dma("gpsimd",
                RB[:, 16384 + 2048 * slot:16384 + 2048 * (slot + 1)].rearrange(
                    "p (k m) -> p k m", m=256),
                w2T_d[ex, :, 256 * dg:256 * (dg + 1)].rearrange("(k p) m -> p k m",
                                                               p=128), sc=sc)
            return sc.n

        def phase_ffn():
            xln_done = sDVE.n
            wait("tensor", sW, init_gp)
            for tm in range(8):
                pe_bank(6)
                wait("tensor", sDVE, xln_done)
                for k in range(NCH):
                    ew("tensor", lambda en, tm=tm, k=k: en.matmul(
                        PS[6][:, 0:8],
                        RA[:, 1024 * k + 128 * tm:1024 * k + 128 * (tm + 1)],
                        GATET[:, 8 * k:8 * (k + 1)],
                        start=(k == 0), stop=(k == NCH - 1)),
                       inc=(sPE if k == NCH - 1 else None))
                wait("scalar", sPE)
                ew("scalar", lambda en: en.activation(SEL[:, 0:8], PS[6][:, 0:8],
                                                      AF.Sigmoid))
                fence("scalar", inc=sACT)
                rel_bank(6, sACT)
                wait("vector", sACT)
                ew("vector", lambda en: en.tensor_add(SEL[:, 8:16], SEL[:, 0:8],
                                                      BIASB[:, :]))
                fence("vector")
                ew("vector", lambda en: en.reduce_max(SEL[:, 16:17], SEL[:, 8:16],
                                                      axis=AX.X, negate=True))
                fence("vector", inc=sDVE)
                wait("scalar", sDVE)
                ew("scalar", lambda en: en.activation(SEL[:, 24:32], SEL[:, 8:16],
                                                      AF.Sign, bias=SEL[:, 16:17]))
                fence("scalar", inc=sACT)
                wait("vector", sACT)
                ew("vector", lambda en: en.tensor_single_scalar(
                    SEL[:, 24:32], SEL[:, 24:32], 1.0, ALU.add))
                fence("vector")
                ew("vector", lambda en: en.scalar_tensor_tensor(
                    SEL[:, 32:40], SEL[:, 24:32], -1e30, SEL[:, 8:16],
                    ALU.mult, ALU.add))
                fence("vector")
                ew("vector", lambda en: en.reduce_max(SEL[:, 19:20], SEL[:, 32:40],
                                                      axis=AX.X, negate=True))
                ew("vector", lambda en: en.reduce_sum(SEL[:, 17:18], SEL[:, 24:32],
                                                      axis=AX.X))
                fence("vector")
                ew("vector", lambda en: en.tensor_single_scalar(
                    SEL[:, 18:19], SEL[:, 17:18], 1.5, ALU.is_lt))
                fence("vector", inc=sDVE)
                wait("scalar", sDVE)
                ew("scalar", lambda en: en.activation(SEL[:, 40:48], SEL[:, 32:40],
                                                      AF.Sign, bias=SEL[:, 19:20]))
                fence("scalar", inc=sACT)
                wait("vector", sACT)
                ew("vector", lambda en: en.tensor_single_scalar(
                    SEL[:, 40:48], SEL[:, 40:48], 1.0, ALU.add))
                fence("vector", inc=sDVE)
                wait("scalar", sDVE)
                ew("scalar", lambda en: en.activation(SEL[:, 48:56], SEL[:, 40:48],
                                                      AF.Copy, scale=SEL[:, 18:19]))
                fence("scalar", inc=sACT)
                wait("vector", sACT)
                ew("vector", lambda en: en.tensor_add(SEL[:, 56:64], SEL[:, 24:32],
                                                      SEL[:, 48:56]))
                fence("vector")
                ew("vector", lambda en: en.tensor_mul(SEL[:, 64:72], SEL[:, 56:64],
                                                      SEL[:, 0:8]))
                fence("vector", inc=sDVE)
                pe_bank(7)
                wait("tensor", sDVE)
                ew("tensor", lambda en: en.transpose(PS[7][0:8, 0:128], SEL[:, 64:72],
                                                     IDF32[:, :]), inc=sPE)
                wait("vector", sPE)
                ew("vector", lambda en, tm=tm: en.tensor_copy(
                    MFM[0:8, 128 * tm:128 * (tm + 1)], PS[7][0:8, 0:128]))
                fence("vector", inc=sDVE)
                rel_bank(7, sDVE)
            sel_done = sDVE.n
            if ffn_mode == "sel":
                return
            nexp_run = {"mbb": 0, "a1": 1, "ab1": 1, "a1slab": 1,
                        "a1mm": 1, "a1mm1": 1, "a1mmne": 1,
                        "a1slab3": 1}.get(ffn_mode, NEXP)

            ex_list = ([8] if ffn_mode == "sh" else
                       [0] if ffn_mode == "e0" else list(range(nexp_run)))
            for ex in ex_list:
                routed = ex < 8
                if routed:
                    pe_bank(4)
                    pe_bank(5)
                    wait("tensor", sDVE, sel_done)
                    for h in range(2):
                        ew("tensor", lambda en, ex=ex, h=h: en.matmul(
                            PS[4 + h][:, :], ONEH[:, 128 * ex:128 * (ex + 1)],
                            MFM[:, 512 * h:512 * (h + 1)], start=True, stop=True),
                           inc=(sPE if h == 1 else None))
                    wait("scalar", sPE)
                    for h in range(2):
                        ew("scalar", lambda en, h=h: en.activation(
                            MBB[:, 512 * h:512 * (h + 1)], PS[4 + h][:, :], AF.Copy),
                           inc=sACT)
                    rel_bank(4, sACT)
                    rel_bank(5, sACT)
                mbb_done = sACT.n
                if ffn_mode == "mbb":
                    continue

                slab_ready = {0: load_slab13(ex, 0), 1: load_slab13(ex, 1)}
                if ffn_mode == "a1slab":
                    wait("tensor", slot_sems[0], slab_ready[0])
                    wait("tensor", slot_sems[1], slab_ready[1])
                    continue
                if ffn_mode == "a1slab3":
                    wait("tensor", slot_sems[0], slab_ready[0])
                    wait("tensor", slot_sems[1], slab_ready[1])
                    slab_guard[("w13", 0)] = sPE.n
                    load_slab13(ex, 2)
                    wait("tensor", slot_sems[0])
                    continue
                ngr = 1 if ffn_mode in ("a1mm1", "a1mmne") else 4
                for mg in range(ngr):
                    slot = mg % 2
                    w1s = RB[:, 4096 * slot:4096 * (slot + 1)]
                    w3s = RB[:, 8192 + 4096 * slot:8192 + 4096 * (slot + 1)]
                    wait("tensor", slot_sems[slot], slab_ready[mg])
                    nhalf = 1 if ffn_mode in ("a1mm1", "a1mmne") else 2
                    for half in range(nhalf):
                        bi = (0, 1, 2, 3) if (2 * mg + half) % 2 == 0 else (4, 5, 6, 7)
                        for b in bi:
                            pe_bank(b)
                        for k in range(NCH):
                            for m in range(2):
                                ew("tensor", lambda en, k=k, m=m, half=half, w1s=w1s,
                                   bi=bi: en.matmul(
                                       PS[bi[m]][:, :],
                                       w1s[:, 256 * k + 128 * m:256 * k + 128 * (m + 1)],
                                       RA[:, 1024 * k + 512 * half:
                                          1024 * k + 512 * (half + 1)],
                                       start=(k == 0), stop=(k == NCH - 1)))
                                ew("tensor", lambda en, k=k, m=m, half=half, w3s=w3s,
                                   bi=bi: en.matmul(
                                       PS[bi[2 + m]][:, :],
                                       w3s[:, 256 * k + 128 * m:256 * k + 128 * (m + 1)],
                                       RA[:, 1024 * k + 512 * half:
                                          1024 * k + 512 * (half + 1)],
                                       start=(k == 0), stop=(k == NCH - 1)),
                                   inc=(sPE if (k == NCH - 1 and m == 1) else None))
                        wait("vector", sPE)
                        wait("scalar", sPE)
                        if ffn_mode == "a1mmne":
                            ew("vector", lambda en: en.engine_nop(), inc=sDVE)
                            for b in bi:
                                bank_rel[b] = None
                            continue
                        if ffn_mode in ("a1mm", "a1mm1"):
                            for m in range(2):
                                hc = 2 * mg + m
                                hdst = RE[:, 1024 * hc + 512 * half:
                                          1024 * hc + 512 * (half + 1)]
                                ew("vector", lambda en, bi=bi, m=m, hdst=hdst:
                                   en.tensor_copy(hdst, PS[bi[m]][:, :]), inc=sDVE)
                            for b in bi:
                                rel_bank(b, sDVE)
                            continue
                        if routed:
                            wait("vector", sACT, mbb_done)
                        for m in range(2):
                            hc = 2 * mg + m
                            hdst = RE[:, 1024 * hc + 512 * half:
                                      1024 * hc + 512 * (half + 1)]
                            qtm = QT[:, 512 * m:512 * (m + 1)]
                            htm = HT[:, 512 * m:512 * (m + 1)]
                            if routed:
                                ew("vector", lambda en, bi=bi, m=m, half=half, htm=htm:
                                   en.tensor_mul(htm, PS[bi[m]][:, :],
                                                 MBB[:, 512 * half:512 * (half + 1)]),
                                   inc=sDVE)
                                wait("scalar", sDVE)
                                ew("scalar", lambda en, qtm=qtm, htm=htm:
                                   en.activation(qtm, htm, AF.Silu), inc=sACT)
                                wait("vector", sACT)
                                ew("vector", lambda en, bi=bi, m=m, half=half, htm=htm:
                                   en.tensor_mul(htm, PS[bi[2 + m]][:, :],
                                                 MBB[:, 512 * half:512 * (half + 1)]))
                                ew("vector", lambda en, qtm=qtm, htm=htm, hdst=hdst:
                                   en.tensor_mul(hdst, qtm, htm), inc=sDVE)
                            else:
                                ew("scalar", lambda en, bi=bi, m=m, qtm=qtm:
                                   en.activation(qtm, PS[bi[m]][:, :], AF.Silu),
                                   inc=sACT)
                                wait("vector", sACT)
                                ew("vector", lambda en, bi=bi, m=m, qtm=qtm, hdst=hdst:
                                   en.tensor_mul(hdst, qtm, PS[bi[2 + m]][:, :]),
                                   inc=sDVE)
                        for b in bi:
                            rel_bank(b, sDVE)
                    slab_guard[("w13", slot)] = sPE.n
                    if mg + 2 < 4:
                        slab_ready[mg + 2] = load_slab13(ex, mg + 2)
                h_done = sDVE.n
                if ffn_mode in ("a1", "a1mm", "a1mm1", "a1mmne", "a1slab3"):
                    continue

                w2_ready = {0: load_slab2(ex, 0), 1: load_slab2(ex, 1)}
                for dg in range(8):
                    slot = dg % 2
                    w2s = RB[:, 16384 + 2048 * slot:16384 + 2048 * (slot + 1)]
                    wait("tensor", slot_sems[slot], w2_ready[dg])
                    wait("tensor", sDVE, h_done)
                    for half in range(2):
                        bi = (0, 1) if (2 * dg + half) % 2 == 0 else (2, 3)
                        for b in bi:
                            pe_bank(b)
                        for k in range(8):
                            for m in range(2):
                                ew("tensor", lambda en, k=k, m=m, half=half, w2s=w2s,
                                   bi=bi: en.matmul(
                                       PS[bi[m]][:, :],
                                       w2s[:, 256 * k + 128 * m:256 * k + 128 * (m + 1)],
                                       RE[:, 1024 * k + 512 * half:
                                          1024 * k + 512 * (half + 1)],
                                       start=(k == 0), stop=(k == 7)),
                                   inc=(sPE if (k == 7 and m == 1) else None))
                        wait("vector", sPE)
                        for m in range(2):
                            dc = 2 * dg + m
                            xc = X[:, 1024 * dc + 512 * half:1024 * dc + 512 * (half + 1)]
                            ew("vector", lambda en, bi=bi, m=m, xc=xc:
                               en.scalar_tensor_tensor(xc, PS[bi[m]][:, :], 0.5, xc,
                                                       ALU.mult, ALU.add), inc=sDVE)
                        for b in bi:
                            rel_bank(b, sDVE)
                    slab_guard[("w2", slot)] = sPE.n
                    if dg + 2 < 8:
                        w2_ready[dg + 2] = load_slab2(ex, dg + 2)

        def phase_qkv():
            xln_done = sDVE.n
            wait("sync", sDVE, xln_done)
            dma("sync", cc1_in.rearrange("(c p) t -> p c t", p=128),
                RA[:, :].rearrange("p (c t) -> p c t", t=TL))
            cc1_staged = sX.n
            wait("gpsimd", sX, cc1_staged)
            ew("gpsimd", lambda en: en.collective_compute(
                "AllGather", ALU.bypass, replica_groups=[list(range(NCORES))],
                ins=[cc1_in], outs=[cc1_out]), inc=sCC)
            ag1_done = sCC.n
            g0 = slab_guard.pop(("w13", 0), None)
            g1 = slab_guard.pop(("w13", 1), None)
            g2a = slab_guard.pop(("w2", 0), None)
            g2b = slab_guard.pop(("w2", 1), None)
            pe_fence = max(x for x in (g0, g1, g2a, g2b, 0) if x is not None)
            wait("sync", sPE, pe_fence)
            dma("sync", RB[0:64, 0:16384], tq_in)
            tq_done = sX.n
            wait("gpsimd", sPE, pe_fence)
            dma("gpsimd", RE[:, 0:6144].rearrange("p (k f) -> p k f", f=384),
                wqkvT_d.rearrange("(k p) f -> p k f", p=128))
            wqkv_done = sW.n

            xg_view = cc1_out.rearrange("(b k p) t -> b p k t", k=NCH, p=128)
            xg_loaded = []
            wait("sync", sCC, ag1_done)
            dma("sync", RA[:, 0:8192].rearrange("p (k t) -> p k t", t=512),
                xg_view[0, :, :, 0:512])
            xg_loaded.append(sX.n)
            pe_after_n = []
            qslot_dve = [0, 0]
            qsq_act = [0, 0]
            hx_dma = [0, 0]
            qa_q = False
            qa_k = False
            for n in range(16):
                slot = n % 2
                if n + 1 < 16:
                    if n >= 1:
                        wait("sync", sPE, pe_after_n[n - 1])
                    nn = n + 1
                    dma("sync",
                        RA[:, 8192 * (nn % 2):8192 * (nn % 2) + 8192].rearrange(
                            "p (k t) -> p k t", t=512),
                        xg_view[nn // 2, :, :, 512 * (nn % 2):512 * (nn % 2) + 512])
                    xg_loaded.append(sX.n)
                bset = (0, 1, 2) if n % 2 == 0 else (3, 4, 5)
                wait("tensor", sX, xg_loaded[n])
                if n == 0:
                    wait("tensor", sW, wqkv_done)
                for m in range(3):
                    pe_bank(bset[m])
                    for k in range(NCH):
                        ew("tensor", lambda en, m=m, k=k, slot=slot, bset=bset:
                           en.matmul(
                               PS[bset[m]][:, :],
                               RE[:, 384 * k + 128 * m:384 * k + 128 * (m + 1)],
                               RA[:, 8192 * slot + 512 * k:8192 * slot + 512 * (k + 1)],
                               start=(k == 0), stop=(k == NCH - 1)),
                           inc=(sPE if k == NCH - 1 else None))
                pe_after_n.append(sPE.n)
                wait("scalar", sPE, pe_after_n[n])
                if n == 0:
                    wait("scalar", sX, tq_done)
                    wait("vector", sX, tq_done)
                for m in range(3):
                    qslot = (3 * n + m) % 2
                    qte = QT[0:64, 512 * qslot:512 * (qslot + 1)]
                    qto = HT[0:64, 512 * qslot:512 * (qslot + 1)]
                    hx = RT[64:128, :] if qslot == 0 else HT[64:128, :]
                    if qslot_dve[qslot]:
                        wait("scalar", sDVE, qslot_dve[qslot])
                    ew("scalar", lambda en, m=m, bset=bset, qte=qte: en.activation(
                        qte, PS[bset[m]][0:64, :], AF.Copy))
                    if m < 2:
                        ew("scalar", lambda en, m=m, bset=bset, qto=qto:
                           en.activation(qto, PS[bset[m]][64:128, :], AF.Copy))
                    else:
                        ew("scalar", lambda en, n=n, m=m, bset=bset: en.activation(
                            KV[64:128, 512 * n:512 * (n + 1)],
                            PS[bset[m]][64:128, :], AF.Copy))
                    fence("scalar", inc=sACT)
                    rel_bank(bset[m], sACT)
                    wait("vector", sACT)
                    tqc = RB[0:64, 512 * n:512 * (n + 1)]
                    tqs = RB[0:64, 8192 + 512 * n:8192 + 512 * (n + 1)]
                    if m < 2:
                        if hx_dma[qslot]:
                            wait("vector", sS, hx_dma[qslot])
                        if qsq_act[qslot]:
                            wait("vector", sACT, qsq_act[qslot])
                        ew("vector", lambda en, qte=qte, tqc=tqc: en.tensor_mul(
                            RT[0:64, 0:512], qte, tqc))
                        ew("vector", lambda en, qto=qto, tqs=tqs: en.tensor_mul(
                            RT[0:64, 512:1024], qto, tqs))
                        ew("vector", lambda en, hx=hx: en.tensor_sub(
                            hx[:, 0:512], RT[0:64, 0:512], RT[0:64, 512:1024]))
                        ew("vector", lambda en, qte=qte, tqs=tqs: en.tensor_mul(
                            RT[0:64, 0:512], qte, tqs))
                        ew("vector", lambda en, qto=qto, tqc=tqc: en.tensor_mul(
                            RT[0:64, 512:1024], qto, tqc))
                        ew("vector", lambda en, hx=hx: en.tensor_add(
                            hx[:, 512:1024], RT[0:64, 0:512], RT[0:64, 512:1024]))
                        fence("vector", inc=sDVE)
                        qslot_dve[qslot] = sDVE.n
                        wait("scalar", sDVE, qslot_dve[qslot])
                        ew("scalar", lambda en, hx=hx: en.activation(
                            QT[64:128, 0:1024], hx[:, :], AF.Square))
                        fence("scalar", inc=sACT)
                        qsq_act[qslot] = sACT.n
                        pe_bank(7)
                        wait("tensor", sACT)
                        ew("tensor", lambda en: en.matmul(
                            PS[7][0:1, :], ONESC[64:128, :], QT[64:128, 0:512],
                            start=True, stop=False))
                        ew("tensor", lambda en: en.matmul(
                            PS[7][0:1, :], ONESC[64:128, :], QT[64:128, 512:1024],
                            start=False, stop=True), inc=sPE)
                        wait("vector", sPE)
                        ew("vector", lambda en: en.reduce_max(
                            QA[0:1, 1:2], PS[7][0:1, :], axis=AX.X))
                        fence("vector")
                        if not qa_q:
                            ew("vector", lambda en: en.tensor_copy(
                                QA[0:1, 0:1], QA[0:1, 1:2]))
                            qa_q = True
                        else:
                            ew("vector", lambda en: en.tensor_max(
                                QA[0:1, 0:1], QA[0:1, 0:1], QA[0:1, 1:2]))
                        fence("vector", inc=sDVE)
                        rel_bank(7, sDVE)
                        dma("scalar",
                            qp_dram[128 * m:128 * (m + 1), 512 * n:512 * (n + 1)]
                            .rearrange("(h p) t -> p h t", h=2),
                            hx[:, :].rearrange("p (h t) -> p h t", h=2))
                        hx_dma[qslot] = sS.n
                    else:
                        kvd = KV[:, 512 * n:512 * (n + 1)]
                        ew("vector", lambda en, qte=qte, tqc=tqc: en.tensor_mul(
                            RT[0:32, 0:512], qte[0:32, :], tqc[0:32, :]))
                        ew("vector", lambda en, qte=qte, tqs=tqs: en.tensor_mul(
                            RT[0:32, 512:1024], qte[32:64, :], tqs[32:64, :]))
                        ew("vector", lambda en, kvd=kvd: en.tensor_sub(
                            kvd[0:32, :], RT[0:32, 0:512], RT[0:32, 512:1024]))
                        ew("vector", lambda en, qte=qte, tqs=tqs: en.tensor_mul(
                            RT[0:32, 0:512], qte[0:32, :], tqs[0:32, :]))
                        ew("vector", lambda en, qte=qte, tqc=tqc: en.tensor_mul(
                            RT[0:32, 512:1024], qte[32:64, :], tqc[32:64, :]))
                        ew("vector", lambda en, kvd=kvd: en.tensor_add(
                            kvd[32:64, :], RT[0:32, 0:512], RT[0:32, 512:1024]))
                        fence("vector", inc=sDVE)
                        qslot_dve[qslot] = sDVE.n
                        wait("scalar", sDVE, qslot_dve[qslot])
                        ew("scalar", lambda en, kvd=kvd: en.activation(
                            QT[64:128, 0:512], kvd[0:64, :], AF.Square))
                        fence("scalar", inc=sACT)
                        qsq_act[qslot] = sACT.n
                        pe_bank(7)
                        wait("tensor", sACT)
                        ew("tensor", lambda en: en.matmul(
                            PS[7][0:1, :], ONESC[64:128, :], QT[64:128, 0:512],
                            start=True, stop=True), inc=sPE)
                        wait("vector", sPE)
                        ew("vector", lambda en: en.reduce_max(
                            QA[0:1, 3:4], PS[7][0:1, :], axis=AX.X))
                        fence("vector")
                        if not qa_k:
                            ew("vector", lambda en: en.tensor_copy(
                                QA[0:1, 2:3], QA[0:1, 3:4]))
                            qa_k = True
                        else:
                            ew("vector", lambda en: en.tensor_max(
                                QA[0:1, 2:3], QA[0:1, 2:3], QA[0:1, 3:4]))
                        fence("vector", inc=sDVE)
                        rel_bank(7, sDVE)
            rope_all_done = sDVE.n
            ew("vector", lambda en: en.tensor_mul(UROW[:, 202:203],
                                                  QA[0:1, 0:1], QA[0:1, 2:3]))
            fence("vector", inc=sDVE)
            wait("scalar", sDVE)
            ew("scalar", lambda en: en.activation(UROW[:, 202:203],
                                                  UROW[:, 202:203], AF.Ln))
            fence("scalar")
            ew("scalar", lambda en: en.activation(UROW[:, 202:203],
                                                  UROW[:, 202:203], AF.Exp,
                                                  scale=0.5))
            fence("scalar", inc=sACT)
            wait("vector", sACT)
            ew("vector", lambda en: en.tensor_single_scalar(
                UROW[:, 203:204], UROW[:, 202:203], -0.125, ALU.mult))
            fence("vector", inc=sDVE)
            u_row_done = sDVE.n
            pe_bank(6)
            wait("tensor", sDVE, u_row_done)
            ew("tensor", lambda en: en.matmul(PS[6][:, 0:1], ONES1[:, :],
                                              UROW[:, 203:204],
                                              start=True, stop=True), inc=sPE)
            wait("vector", sPE)
            ew("vector", lambda en: en.tensor_copy(NEGUB[:, :], PS[6][:, 0:1]))
            fence("vector", inc=sDVE)
            rel_bank(6, sDVE)

        def phase_attn():
            kv_done = sDVE.n
            qp_stores = sS.n
            ew("vector", lambda en: en.memset(VAUG[:, :], 1.0), inc=sDVE)
            vaug_ones = sDVE.n
            for kt in range(64):
                b = 4 + kt % 2
                pe_bank(b)
                if kt == 0:
                    wait("tensor", sDVE, kv_done)
                ew("tensor", lambda en, kt=kt, b=b: en.transpose(
                    PS[b].bitcast(BF16)[:, 0:64],
                    KV[64:128, 128 * kt:128 * (kt + 1)],
                    IDBF[64:128, 64:128]), inc=sPE)
                wait("vector", sPE)
                if kt == 0:
                    wait("vector", sDVE, vaug_ones)
                ew("vector", lambda en, kt=kt, b=b: en.tensor_copy(
                    VAUG[:, 65 * kt:65 * kt + 64],
                    PS[b].bitcast(BF16)[:, 0:64]), inc=sDVE)
                rel_bank(b, sDVE)
            vaug_done = sDVE.n

            qext = RE[0:64, :]
            pt_slots = [QT[:, 0:1024], HT[:, 0:1024]]
            wait("scalar", sS, qp_stores)
            wait("vector", sS, qp_stores)
            y_store = []
            for h in range(HLOC):
                p = h // 2
                isB = h % 2 == 1
                wait("sync", sPE)
                if h == 0:
                    wait("sync", sS, qp_stores)
                re_ = 128 * p + (32 if isB else 0)
                ro_ = 128 * p + 64 + (32 if isB else 0)
                dma("sync", RE[0:32, :], qp_dram[re_:re_ + 32, :])
                dma("sync", RE[32:64, :], qp_dram[ro_:ro_ + 32, :])
                q_ready = sX.n
                wait("tensor", sX, q_ready)
                if h == 0:
                    wait("tensor", sDVE, vaug_done)
                praw_dve = 0
                pts = [QT[:, 0:512], QT[:, 512:1024], HT[:, 0:512],
                       HT[:, 512:1024]]
                for qc in range(16):
                    accb = 6 + qc % 2
                    ktmax = 4 * qc + 4
                    pe_bank(accb)
                    av_pe = []
                    s_pe = []

                    def emit_s(kt, qc=qc):
                        sb = kt % 4
                        pe_bank(sb)
                        ew("tensor", lambda en, kt=kt, qc=qc, sb=sb: en.matmul(
                            PS[sb], KV[0:64, 128 * kt:128 * (kt + 1)],
                            qext[:, 512 * qc:512 * (qc + 1)],
                            start=True, stop=True), inc=sPE)
                        s_pe.append(sPE.n)

                    emit_s(0)
                    emit_s(1)
                    for kt in range(ktmax):
                        sb = kt % 4
                        diag = kt >= 4 * qc
                        wait("scalar", sPE, s_pe[kt])
                        if kt >= 4:
                            wait("scalar", sPE, av_pe[kt - 4])
                        if diag and praw_dve:
                            wait("scalar", sDVE, praw_dve)
                        dst = RT[:, 0:512] if diag else pts[sb]
                        ew("scalar", lambda en, sb=sb, dst=dst: en.activation(
                            dst, PS[sb], AF.Exp, bias=NEGUB[:, 0:1],
                            scale=0.125), inc=sACT)
                        rel_bank(sb, sACT)
                        if diag:
                            r = kt - 4 * qc
                            wait("vector", sACT)
                            if kt >= 4:
                                wait("vector", sPE, av_pe[kt - 4])
                            ew("vector", lambda en, sb=sb, r=r: en.tensor_mul(
                                pts[sb], RT[:, 0:512],
                                M01S[:, 512 * r:512 * (r + 1)]), inc=sDVE)
                            praw_dve = sDVE.n
                        if kt + 2 < ktmax:
                            emit_s(kt + 2)
                        if diag:
                            wait("tensor", sDVE, praw_dve)
                        else:
                            wait("tensor", sACT)
                        ew("tensor", lambda en, kt=kt, sb=sb, accb=accb, ktmax=ktmax:
                           en.matmul(
                               PS[accb][0:65, :], VAUG[:, 65 * kt:65 * kt + 65],
                               pts[sb],
                               start=(kt == 0), stop=(kt == ktmax - 1),
                               skip_group_check=True), inc=sPE)
                        av_pe.append(sPE.n)
                    wait("scalar", sPE, av_pe[-1])
                    ew("scalar", lambda en, accb=accb: en.activation(
                        ACCS[0:65, :], PS[accb][0:65, :], AF.Copy))
                    fence("scalar", inc=sACT)
                    rel_bank(accb, sACT)
                    wait("vector", sACT)
                    ew("vector", lambda en: en.reciprocal(REC, ACCS[64:65, :]))
                    fence("vector", inc=sDVE)
                    pe_bank(2)
                    wait("tensor", sDVE)
                    ew("tensor", lambda en: en.matmul(PS[2][0:64, :],
                                                      ONES1[0:1, 0:64], REC,
                                                      start=True, stop=True), inc=sPE)
                    yslot = qc % 2
                    if len(y_store) >= 2:
                        wait("vector", sS, y_store[-2])
                    wait("vector", sPE)
                    ew("vector", lambda en, yslot=yslot: en.tensor_mul(
                        YO[:, 512 * yslot:512 * (yslot + 1)], ACCS[0:64, :],
                        PS[2][0:64, :]), inc=sDVE)
                    rel_bank(2, sDVE)
                    wait("scalar", sDVE)
                    dma("scalar", cc2_in[64 * h:64 * (h + 1), 512 * qc:512 * (qc + 1)],
                        YO[:, 512 * yslot:512 * (yslot + 1)])
                    y_store.append(sS.n)
            wait("gpsimd", sS, y_store[-1])
            ew("gpsimd", lambda en: en.collective_compute(
                "AllGather", ALU.bypass, replica_groups=[list(range(NCORES))],
                ins=[cc2_in], outs=[cc2_out]), inc=sCC)

        def phase_wo():
            ag2_done = sCC.n
            wait("gpsimd", sCC, ag2_done)
            wait("gpsimd", sPE)
            src = cc2_out.rearrange("f (b t) -> (f b) t", b=NCORES)
            gdone = []
            for k in range(NCH):
                ew("gpsimd", lambda en, k=k: en.indirect_dma_start(
                    out=RA[:, 1024 * k:1024 * (k + 1)],
                    out_offset=None,
                    in_=src,
                    in_offset=IndirectOffsetOnAxis(WOIDX[:, k:k + 1], 0)),
                   inc=sW, k=16)
                gdone.append(sW.n)
            wo_ready = {}
            for mg in range(2):
                slot = mg % 2
                dma("gpsimd",
                    RB[:, 4096 * slot:4096 * (slot + 1)].rearrange(
                        "p (k m) -> p k m", m=256),
                    woT_d[:, 256 * mg:256 * (mg + 1)].rearrange(
                        "(k p) m -> p k m", p=128), sc=slot_sems[slot])
                wo_ready[mg] = slot_sems[slot].n
            for mg in range(8):
                slot = mg % 2
                wos = RB[:, 4096 * slot:4096 * (slot + 1)]
                wait("tensor", slot_sems[slot], wo_ready[mg])
                wait("tensor", sW, gdone[-1])
                for half in range(2):
                    bi = (0, 1) if (2 * mg + half) % 2 == 0 else (2, 3)
                    for b in bi:
                        pe_bank(b)
                    for k in range(NCH):
                        for m in range(2):
                            ew("tensor", lambda en, k=k, m=m, half=half, wos=wos,
                               bi=bi: en.matmul(
                                   PS[bi[m]][:, :],
                                   wos[:, 256 * k + 128 * m:256 * k + 128 * (m + 1)],
                                   RA[:, 1024 * k + 512 * half:
                                      1024 * k + 512 * (half + 1)],
                                   start=(k == 0), stop=(k == NCH - 1)),
                               inc=(sPE if (k == NCH - 1 and m == 1) else None))
                    wait("vector", sPE)
                    for m in range(2):
                        dc = 2 * mg + m
                        xc = X[:, 1024 * dc + 512 * half:1024 * dc + 512 * (half + 1)]
                        ew("vector", lambda en, bi=bi, m=m, xc=xc: en.tensor_add(
                            xc, xc, PS[bi[m]][:, :]), inc=sDVE)
                    for b in bi:
                        rel_bank(b, sDVE)
                if mg + 2 < 8:
                    wait("gpsimd", sPE)
                    dma("gpsimd",
                        RB[:, 4096 * slot:4096 * (slot + 1)].rearrange(
                            "p (k m) -> p k m", m=256),
                        woT_d[:, 256 * (mg + 2):256 * (mg + 3)].rearrange(
                            "(k p) m -> p k m", p=128), sc=slot_sems[slot])
                    wo_ready[mg + 2] = slot_sems[slot].n
            slab_guard[("w13", 0)] = sPE.n
            slab_guard[("w13", 1)] = sPE.n
            slab_guard[("w2", 0)] = sPE.n
            slab_guard[("w2", 1)] = sPE.n

        stop = PHASES.index(upto)
        phase_fns = {"ln1": phase_ln, "ffn1": phase_ffn, "ln2": phase_ln,
                     "qkv": phase_qkv, "attn": phase_attn, "wo": phase_wo,
                     "ln3": phase_ln, "ffn2": phase_ffn}
        for i, name in enumerate(PHASES):
            if i > stop:
                break
            phase_fns[name]()

        if "xln" in tap_aps or "tap_xln" in tap_aps:
            pass
        if "tap_xln" in tap_aps:
            wait("sync", sDVE)
            dma("sync", tap_aps["tap_xln"], RA[:, :])
        if "tap_re" in tap_aps:
            wait("sync", sDVE)
            dma("sync", tap_aps["tap_re"], RE[:, :])
        if "tap_mfm" in tap_aps:
            wait("sync", sDVE)
            dma("sync", tap_aps["tap_mfm"], MFM[:, :])
        if "tap_negub" in tap_aps:
            wait("sync", sDVE)
            dma("sync", tap_aps["tap_negub"], NEGUB[:, :])
            dma("sync", tap_aps["tap_urow"], ROWS[0:1, 512:1024])
        if "tap_mbb" in tap_aps:
            wait("sync", sACT)
            dma("sync", tap_aps["tap_mbb"], MBB[:, :])
        if "tap_kv" in tap_aps:
            wait("sync", sDVE)
            dma("sync", tap_aps["tap_kv"], KV[:, :])
        if "tap_vaug" in tap_aps:
            wait("sync", sDVE)
            dma("sync", tap_aps["tap_vaug"], VAUG[:, :])
        if "tap_qp" in tap_aps:
            wait("sync", sS)
            wait("sync", sDVE)
            for half in range(4):
                dma("sync", RA[:, 0:4096], qp_dram[:, :].rearrange(
                    "(a p) t -> p a t", p=128)[:, :, 2048 * half:2048 * (half + 1)])
                wait("sync", sX)
                dma("sync", tap_aps["tap_qp"].rearrange("(a p) t -> p a t", p=128)
                    [:, :, 2048 * half:2048 * (half + 1)], RA[:, 0:4096])
                wait("sync", sX)
        if "tap_cc2" in tap_aps:
            wait("sync", sCC)
            for half in range(4):
                dma("sync", RA[:, 0:4096], cc2_in.rearrange(
                    "(a p) t -> p a t", p=128)[:, :, 2048 * half:2048 * (half + 1)])
                wait("sync", sX)
                dma("sync", tap_aps["tap_cc2"].rearrange("(a p) t -> p a t", p=128)
                    [:, :, 2048 * half:2048 * (half + 1)], RA[:, 0:4096])
                wait("sync", sX)

        wait("sync", sDVE)
        wait("sync", sPE)
        dma("sync", out_x.rearrange("(c p) t -> p c t", p=128),
            X[:, :].rearrange("p (c t) -> p c t", t=TL))
        wait("sync", sX)

        @block.sync
        def _(en):
            for f in prog["sync"]:
                f(en)

        @block.scalar
        def _(en):
            for f in prog["scalar"]:
                f(en)

        @block.vector
        def _(en):
            for f in prog["vector"]:
                f(en)

        @block.tensor
        def _(en):
            for f in prog["tensor"]:
                f(en)

        @block.gpsimd
        def _(en):
            for f in prog["gpsimd"]:
                f(en)

    return nc



def host_prep(inputs):
    x = np.asarray(inputs["L0_attn_x"], dtype=np.float32).reshape(T, D)
    wq = np.asarray(inputs["L0_attn_wq"], dtype=np.float32)
    wk = np.asarray(inputs["L0_attn_wk"], dtype=np.float32)
    wv = np.asarray(inputs["L0_attn_wv"], dtype=np.float32)
    wo = np.asarray(inputs["L0_attn_wo"], dtype=np.float32)
    eb = np.asarray(inputs["L0_ffn_expert_bias"], dtype=np.float32)
    gw = np.asarray(inputs["L0_ffn_gate_w"], dtype=np.float32)
    w1 = np.asarray(inputs["L0_ffn_w1"], dtype=np.float32)
    w2 = np.asarray(inputs["L0_ffn_w2"], dtype=np.float32)
    w3 = np.asarray(inputs["L0_ffn_w3"], dtype=np.float32)
    sw1 = np.asarray(inputs["L0_ffn_sw1"], dtype=np.float32)
    sw2 = np.asarray(inputs["L0_ffn_sw2"], dtype=np.float32)
    sw3 = np.asarray(inputs["L0_ffn_sw3"], dtype=np.float32)

    C = np.ascontiguousarray

    w1T = np.stack([C(w1[i].T) for i in range(8)] + [C(sw1.T)])
    w3T = np.stack([C(w3[i].T) for i in range(8)] + [C(sw3.T)])
    w2T = np.stack([C(w2[i].T) for i in range(8)] + [C(sw2.T)])
    gateT = C(gw.T)
    biasT = np.tile(eb[None, :], (128, 1)).astype(np.float32)
    woT = C(wo.T)

    theta = 1.0 / (10000.0 ** (np.arange(0, HD, 2, dtype=np.float32) / HD))
    ang = np.arange(T, dtype=np.float32)[:, None] * theta[None, :]
    cosT = np.cos(ang).T.astype(np.float32)
    sinT = np.sin(ang).T.astype(np.float32)
    cos2 = np.concatenate([cosT, cosT], axis=0)
    sin2 = np.concatenate([sinT, sinT], axis=0)
    tq = np.concatenate([cos2, sin2], axis=1).astype(BF)

    kk = np.arange(128)[:, None]
    qq = np.arange(512)[None, :]
    m01 = np.concatenate([(kk <= qq - 128 * r) for r in range(4)],
                         axis=1).astype(BF)

    idbf = np.eye(128, dtype=BF)
    idf32 = np.eye(128, dtype=np.float32)
    ones1 = np.ones((1, 128), dtype=np.float32)
    onescol = np.ones((128, 1), dtype=BF)
    oneh = np.zeros((8, 1024), dtype=BF)
    for ex in range(8):
        oneh[ex, 128 * ex:128 * (ex + 1)] = 1.0

    per_core = []
    for c in range(NCORES):
        xs = C(x[TL * c:TL * (c + 1)].T)
        rows = []
        for pair in range(2):
            hA = 4 * c + 2 * pair
            hB = hA + 1
            qa = wq[HD * hA:HD * (hA + 1)]
            qb = wq[HD * hB:HD * (hB + 1)]
            rows += [qa[0::2], qb[0::2], qa[1::2], qb[1::2]]
        kc = wk[HD * c:HD * (c + 1)]
        rows += [kc[0::2], kc[1::2], wv[HD * c:HD * (c + 1)]]
        wqkvT = C(np.concatenate(rows, axis=0).T)
        woidx = ((128 * np.arange(NCH)[None, :] + np.arange(128)[:, None]) * NCORES
                 + c).astype(np.int32)
        per_core.append({
            "x_in": xs, "wqkvT": wqkvT, "woT": woT,
            "w1T": w1T, "w3T": w3T, "w2T": w2T,
            "gateT": gateT, "biasT": biasT,
            "tq_in": tq, "m01_in": m01,
            "idbf_in": idbf, "idf32_in": idf32, "ones_in": ones1,
            "onescol_in": onescol, "oneh_in": oneh, "woidx_in": woidx,
        })
    return per_core


def kernel(**inputs):
    per_core = host_prep(inputs)
    nc = build_nc()
    res = run_bass_kernel_spmd(nc, per_core, core_ids=list(range(NCORES)))
    outs = [res.results[c]["out_x"] for c in range(NCORES)]
    full = np.concatenate([o.T for o in outs], axis=0)
    return full.reshape(1, T, D).astype(np.float32)


# version 07 (final)
# speedup vs baseline: 1.1331x; 1.1147x over previous
import numpy as np
import ml_dtypes
from contextlib import ExitStack

import concourse.bass as bass
import concourse.mybir as mybir
from concourse.bass_utils import run_bass_kernel_spmd
from concourse.bass_core_rust import IndirectOffsetOnAxis

F32 = mybir.dt.float32
BF16 = mybir.dt.bfloat16
I32 = mybir.dt.int32
AF = mybir.ActivationFunctionType
ALU = mybir.AluOpType
AX = mybir.AxisListType
BF = ml_dtypes.bfloat16

D = 2048
T = 8192
NCORES = 8
TL = T // NCORES
NCH = D // 128
HID = 1024
NEXP = 9
HLOC = 4
HD = 64
EPS = 1e-5

PHASES = ("ln1", "ffn1", "ln2", "qkv", "attn", "wo", "ln3", "ffn2")


class Ctr:
    def __init__(self, handle):
        self.h = handle
        self.n = 0


def build_nc(upto="ffn2", taps=(), ffn_mode="full"):
    nc = bass.Bass("TRN2", target_bir_lowering=False)

    x_in = nc.dram_tensor("x_in", [D, TL], F32, kind="ExternalInput").ap()
    wqkvT_d = nc.dram_tensor("wqkvT", [D, 384], F32, kind="ExternalInput").ap()
    woT_d = nc.dram_tensor("woT", [D, D], F32, kind="ExternalInput").ap()
    w1T_d = nc.dram_tensor("w1T", [NEXP, D, HID], F32, kind="ExternalInput").ap()
    w3T_d = nc.dram_tensor("w3T", [NEXP, D, HID], F32, kind="ExternalInput").ap()
    w2T_d = nc.dram_tensor("w2T", [NEXP, HID, D], F32, kind="ExternalInput").ap()
    gateT_d = nc.dram_tensor("gateT", [D, 8], F32, kind="ExternalInput").ap()
    biasT_d = nc.dram_tensor("biasT", [128, 8], F32, kind="ExternalInput").ap()
    tq_in = nc.dram_tensor("tq_in", [64, 2 * T], BF16, kind="ExternalInput").ap()
    m01_in = nc.dram_tensor("m01_in", [128, 2048], BF16, kind="ExternalInput").ap()
    idbf_in = nc.dram_tensor("idbf_in", [128, 128], BF16, kind="ExternalInput").ap()
    idf32_in = nc.dram_tensor("idf32_in", [128, 128], F32, kind="ExternalInput").ap()
    ones_in = nc.dram_tensor("ones_in", [1, 128], F32, kind="ExternalInput").ap()
    onescol_in = nc.dram_tensor("onescol_in", [128, 1], BF16, kind="ExternalInput").ap()
    oneh_in = nc.dram_tensor("oneh_in", [8, 1024], BF16, kind="ExternalInput").ap()
    woidx_in = nc.dram_tensor("woidx_in", [128, 16], I32, kind="ExternalInput").ap()

    out_x = nc.dram_tensor("out_x", [D, TL], F32, kind="ExternalOutput").ap()
    tap_aps = {}
    if "xln" in taps:
        tap_aps["tap_xln"] = nc.dram_tensor("tap_xln", [128, 16384], BF16,
                                            kind="ExternalOutput").ap()
    if "qp" in taps:
        tap_aps["tap_qp"] = nc.dram_tensor("tap_qp", [256, T], BF16,
                                           kind="ExternalOutput").ap()
    if "kv" in taps:
        tap_aps["tap_kv"] = nc.dram_tensor("tap_kv", [128, T], BF16,
                                           kind="ExternalOutput").ap()
    if "cc2" in taps:
        tap_aps["tap_cc2"] = nc.dram_tensor("tap_cc2", [256, T], BF16,
                                            kind="ExternalOutput").ap()
    if "negub" in taps:
        tap_aps["tap_negub"] = nc.dram_tensor("tap_negub", [128, 1], F32,
                                              kind="ExternalOutput").ap()
        tap_aps["tap_urow"] = nc.dram_tensor("tap_urow", [1, 512], F32,
                                             kind="ExternalOutput").ap()
    if "re" in taps:
        tap_aps["tap_re"] = nc.dram_tensor("tap_re", [128, 8192], BF16,
                                           kind="ExternalOutput").ap()
    if "mfm" in taps:
        tap_aps["tap_mfm"] = nc.dram_tensor("tap_mfm", [8, 1024], BF16,
                                            kind="ExternalOutput").ap()
    if "mbb" in taps:
        tap_aps["tap_mbb"] = nc.dram_tensor("tap_mbb", [128, 1024], BF16,
                                            kind="ExternalOutput").ap()
    if "vaug" in taps:
        tap_aps["tap_vaug"] = nc.dram_tensor("tap_vaug", [128, 65 * 64], BF16,
                                             kind="ExternalOutput").ap()

    cc1_in = nc.dram_tensor("cc1_in", [D, TL], BF16).ap()
    cc1_out = nc.dram_tensor("cc1_out", [NCORES * D, TL], BF16).ap()
    cc2_in = nc.dram_tensor("cc2_in", [256, T], BF16).ap()
    cc2_out = nc.dram_tensor("cc2_out", [D, T], BF16).ap()
    qp_dram = nc.dram_tensor("qp_dram", [256, T], BF16).ap()

    with ExitStack() as st:
        e = st.enter_context
        X = e(nc.sbuf_tensor("X", [128, NCH * TL], F32))
        RA = e(nc.sbuf_tensor("RA", [128, 16384], BF16))
        RB = e(nc.sbuf_tensor("RB", [128, 20480], BF16))
        KV = e(nc.sbuf_tensor("KV", [128, T], BF16))
        RE = e(nc.sbuf_tensor("RE", [128, 8192], BF16))
        VAUG = e(nc.sbuf_tensor("VAUG", [128, 65 * 64], BF16))
        M01S = e(nc.sbuf_tensor("M01S", [128, 2048], BF16))
        ROWS = e(nc.sbuf_tensor("ROWS", [1, 1536], F32))
        MFM = e(nc.sbuf_tensor("MFM", [8, 1024], BF16))
        SEL = e(nc.sbuf_tensor("SEL", [128, 80], F32))
        MBB = e(nc.sbuf_tensor("MBB", [128, 1024], BF16))
        HT = e(nc.sbuf_tensor("HT", [128, 1024], BF16))
        QT = e(nc.sbuf_tensor("QT", [128, 1024], BF16))
        RT = e(nc.sbuf_tensor("RT", [128, 1024], BF16))
        YO = e(nc.sbuf_tensor("YO", [64, 1024], BF16))
        ACCS = e(nc.sbuf_tensor("ACCS", [128, 512], F32))
        QA = e(nc.sbuf_tensor("QA", [128, 4], F32))
        NEGUB = e(nc.sbuf_tensor("NEGUB", [128, 1], F32))
        IDBF = e(nc.sbuf_tensor("IDBF", [128, 128], BF16))
        IDF32 = e(nc.sbuf_tensor("IDF32", [128, 128], F32))
        ONES1 = e(nc.sbuf_tensor("ONES1", [1, 128], F32))
        ONESC = e(nc.sbuf_tensor("ONESC", [128, 1], BF16))
        GATET = e(nc.sbuf_tensor("GATET", [128, 128], BF16))
        BIASB = e(nc.sbuf_tensor("BIASB", [128, 8], F32))
        ONEH = e(nc.sbuf_tensor("ONEH", [8, 1024], BF16))
        WOIDX = e(nc.sbuf_tensor("WOIDX", [128, 16], I32))

        REC = ROWS[0:1, 0:512]
        LNT = None
        UROW = ROWS[0:1, 512:1024]
        PSW = [e(nc.psum_tensor(f"psw{i}", [128, 1024], F32)) for i in range(4)]
        PS = [PSW[i // 2][:, 512 * (i % 2):512 * (i % 2 + 1)] for i in range(8)]

        sX = Ctr(e(nc.semaphore("sX")))
        sS = Ctr(e(nc.semaphore("sS")))
        sW = Ctr(e(nc.semaphore("sW")))
        sA0 = Ctr(e(nc.semaphore("sA0")))
        sA1 = Ctr(e(nc.semaphore("sA1")))
        sPE = Ctr(e(nc.semaphore("sPE")))
        sDVE = Ctr(e(nc.semaphore("sDVE")))
        sACT = Ctr(e(nc.semaphore("sACT")))
        sCC = Ctr(e(nc.semaphore("sCC")))
        block = e(nc.Block())

        prog = {k: [] for k in ("sync", "scalar", "vector", "tensor", "gpsimd")}

        def ew(eng, fn, inc=None, k=1):
            if inc is not None:
                inc.n += k
                def f(en, fn=fn, h=inc.h, k=k):
                    fn(en).then_inc(h, k)
            else:
                def f(en, fn=fn):
                    fn(en)
            prog[eng].append(f)

        def wait(eng, sc, n=None):
            tgt = sc.n if n is None else n
            prog[eng].append(lambda en, h=sc.h, v=tgt: en.wait_ge(h, v))

        def dma(eng, out, in_, sc=None, **kw):
            sc = sc if sc is not None else {"sync": sX, "scalar": sS, "gpsimd": sW}[eng]
            ew(eng, lambda en, o=out, i=in_, kw=kw: en.dma_start(out=o, in_=i, **kw),
               inc=sc, k=16)
            return sc.n

        def fence(eng, inc=None, k=1):
            ew(eng, lambda en: en.drain(), inc=inc, k=k)

        bank_rel = [None] * 8

        def pe_bank(b):
            r = bank_rel[b]
            if r is not None:
                wait("tensor", r[0], r[1])
                bank_rel[b] = None

        def rel_bank(b, sc):
            bank_rel[b] = (sc, sc.n)

        dma("sync", X[:, :].rearrange("p (c t) -> p c t", t=TL),
            x_in.rearrange("(c p) t -> p c t", p=128))
        dma("sync", IDBF[:, :], idbf_in)
        dma("sync", IDF32[:, :], idf32_in)
        dma("sync", ONES1[:, :], ones_in)
        dma("sync", ONESC[:, :], onescol_in)
        dma("sync", BIASB[:, :], biasT_d)
        dma("sync", ONEH[:, :], oneh_in)
        dma("sync", WOIDX[:, :], woidx_in)
        dma("sync", M01S[:, :], m01_in)
        init_sync = sX.n
        dma("gpsimd", GATET[:, :].rearrange("p (k e) -> p k e", e=8),
            gateT_d.rearrange("(k p) e -> p k e", p=128))
        init_gp = sW.n

        def phase_ln():
            wait("vector", sX, init_sync)
            wait("scalar", sX, init_sync)
            wait("scalar", sDVE)
            for b in range(4):
                pe_bank(b)
            pe_after_mm = []
            cslots = [HT, RT]
            qslots = [QT, MBB]
            for c in range(NCH):
                sl = c % 2
                if c >= 2:
                    wait("vector", sPE, pe_after_mm[c - 2])
                    wait("scalar", sPE, pe_after_mm[c - 2])
                ew("vector", lambda en, c=c, sl=sl: en.tensor_copy(
                    cslots[sl][:, :], X[:, 1024 * c:1024 * (c + 1)]), inc=sDVE)
                ew("scalar", lambda en, c=c, sl=sl: en.activation(
                    qslots[sl][:, :], X[:, 1024 * c:1024 * (c + 1)], AF.Square),
                   inc=sACT)
                wait("tensor", sDVE)
                wait("tensor", sACT)
                for h in range(2):
                    ew("tensor", lambda en, c=c, sl=sl, h=h: en.matmul(
                        PS[h][0:1, :], ONESC[:, :],
                        cslots[sl][:, 512 * h:512 * (h + 1)],
                        start=(c == 0), stop=(c == NCH - 1)))
                    ew("tensor", lambda en, c=c, sl=sl, h=h: en.matmul(
                        PS[2 + h][0:1, :], ONESC[:, :],
                        qslots[sl][:, 512 * h:512 * (h + 1)],
                        start=(c == 0), stop=(c == NCH - 1)), inc=sPE)
                pe_after_mm.append(sPE.n)
            pe_stats_done = sPE.n
            r_rstd = ROWS[0:1, 0:512]
            r_mr = ROWS[0:1, 512:1024]
            r_scr = ROWS[0:1, 1024:1536]
            for h in range(2):
                wait("vector", sPE, pe_stats_done)
                ew("vector", lambda en, h=h: en.tensor_scalar_mul(
                    r_scr, PS[h][0:1, :], 1.0 / D))
                fence("vector")
                ew("vector", lambda en: en.tensor_mul(r_mr, r_scr, r_scr))
                ew("vector", lambda en, h=h: en.tensor_scalar_mul(
                    r_rstd, PS[2 + h][0:1, :], 1.0 / D))
                fence("vector")
                ew("vector", lambda en: en.tensor_sub(r_rstd, r_rstd, r_mr))
                fence("vector")
                ew("vector", lambda en: en.tensor_scalar_add(r_rstd, r_rstd, EPS))
                fence("vector", inc=sDVE)
                wait("scalar", sDVE)
                ew("scalar", lambda en: en.activation(r_rstd, r_rstd, AF.Ln))
                fence("scalar")
                ew("scalar", lambda en: en.activation(r_rstd, r_rstd, AF.Exp,
                                                      scale=-0.5))
                fence("scalar", inc=sACT)
                wait("vector", sACT)
                ew("vector", lambda en: en.tensor_mul(r_mr, r_scr, r_rstd))
                fence("vector", inc=sDVE)
                if h == 1:
                    for b in range(4):
                        rel_bank(b, sDVE)
                pe_bank(4)
                pe_bank(5)
                wait("tensor", sDVE)
                ew("tensor", lambda en: en.matmul(PS[4][:, :], ONES1[:, :], r_rstd,
                                                  start=True, stop=True))
                ew("tensor", lambda en: en.matmul(PS[5][:, :], ONES1[:, :], r_mr,
                                                  start=True, stop=True), inc=sPE)
                wait("vector", sPE)
                for c in range(NCH):
                    ew("vector", lambda en, c=c, h=h: en.tensor_mul(
                        ACCS[:, :], X[:, 1024 * c + 512 * h:1024 * c + 512 * (h + 1)],
                        PS[4][:, :]))
                    ew("vector", lambda en, c=c, h=h: en.tensor_sub(
                        RA[:, 1024 * c + 512 * h:1024 * c + 512 * (h + 1)],
                        ACCS[:, :], PS[5][:, :]), inc=sDVE)
                rel_bank(4, sDVE)
                rel_bank(5, sDVE)

        slot_sems = [sA0, sA1]
        slab_guard = {}

        def load_slab13(ex, mg):
            slot = mg % 2
            sc = slot_sems[slot]
            g = slab_guard.pop(("w13", slot), None)
            if g is not None:
                wait("gpsimd", sPE, g)
            dma("gpsimd",
                RB[:, 4096 * slot:4096 * (slot + 1)].rearrange("p (k m) -> p k m",
                                                               m=256),
                w1T_d[ex, :, 256 * mg:256 * (mg + 1)].rearrange("(k p) m -> p k m",
                                                                p=128), sc=sc)
            dma("gpsimd",
                RB[:, 8192 + 4096 * slot:8192 + 4096 * (slot + 1)].rearrange(
                    "p (k m) -> p k m", m=256),
                w3T_d[ex, :, 256 * mg:256 * (mg + 1)].rearrange("(k p) m -> p k m",
                                                               p=128), sc=sc)
            return sc.n

        def load_slab2(ex, dg):
            slot = dg % 2
            sc = slot_sems[slot]
            g = slab_guard.pop(("w2", slot), None)
            if g is not None:
                wait("gpsimd", sPE, g)
            dma("gpsimd",
                RB[:, 16384 + 2048 * slot:16384 + 2048 * (slot + 1)].rearrange(
                    "p (k m) -> p k m", m=256),
                w2T_d[ex, :, 256 * dg:256 * (dg + 1)].rearrange("(k p) m -> p k m",
                                                               p=128), sc=sc)
            return sc.n

        def phase_ffn():
            xln_done = sDVE.n
            wait("tensor", sW, init_gp)
            for tm in range(8):
                pe_bank(6)
                wait("tensor", sDVE, xln_done)
                for k in range(NCH):
                    ew("tensor", lambda en, tm=tm, k=k: en.matmul(
                        PS[6][:, 0:8],
                        RA[:, 1024 * k + 128 * tm:1024 * k + 128 * (tm + 1)],
                        GATET[:, 8 * k:8 * (k + 1)],
                        start=(k == 0), stop=(k == NCH - 1)),
                       inc=(sPE if k == NCH - 1 else None))
                wait("scalar", sPE)
                ew("scalar", lambda en: en.activation(SEL[:, 0:8], PS[6][:, 0:8],
                                                      AF.Sigmoid))
                fence("scalar", inc=sACT)
                rel_bank(6, sACT)
                wait("vector", sACT)
                ew("vector", lambda en: en.tensor_add(SEL[:, 8:16], SEL[:, 0:8],
                                                      BIASB[:, :]))
                fence("vector")
                ew("vector", lambda en: en.reduce_max(SEL[:, 16:17], SEL[:, 8:16],
                                                      axis=AX.X, negate=True))
                fence("vector", inc=sDVE)
                wait("scalar", sDVE)
                ew("scalar", lambda en: en.activation(SEL[:, 24:32], SEL[:, 8:16],
                                                      AF.Sign, bias=SEL[:, 16:17]))
                fence("scalar", inc=sACT)
                wait("vector", sACT)
                ew("vector", lambda en: en.tensor_single_scalar(
                    SEL[:, 24:32], SEL[:, 24:32], 1.0, ALU.add))
                fence("vector")
                ew("vector", lambda en: en.scalar_tensor_tensor(
                    SEL[:, 32:40], SEL[:, 24:32], -1e30, SEL[:, 8:16],
                    ALU.mult, ALU.add))
                fence("vector")
                ew("vector", lambda en: en.reduce_max(SEL[:, 19:20], SEL[:, 32:40],
                                                      axis=AX.X, negate=True))
                ew("vector", lambda en: en.reduce_sum(SEL[:, 17:18], SEL[:, 24:32],
                                                      axis=AX.X))
                fence("vector")
                ew("vector", lambda en: en.tensor_single_scalar(
                    SEL[:, 18:19], SEL[:, 17:18], 1.5, ALU.is_lt))
                fence("vector", inc=sDVE)
                wait("scalar", sDVE)
                ew("scalar", lambda en: en.activation(SEL[:, 40:48], SEL[:, 32:40],
                                                      AF.Sign, bias=SEL[:, 19:20]))
                fence("scalar", inc=sACT)
                wait("vector", sACT)
                ew("vector", lambda en: en.tensor_single_scalar(
                    SEL[:, 40:48], SEL[:, 40:48], 1.0, ALU.add))
                fence("vector", inc=sDVE)
                wait("scalar", sDVE)
                ew("scalar", lambda en: en.activation(SEL[:, 48:56], SEL[:, 40:48],
                                                      AF.Copy, scale=SEL[:, 18:19]))
                fence("scalar", inc=sACT)
                wait("vector", sACT)
                ew("vector", lambda en: en.tensor_add(SEL[:, 56:64], SEL[:, 24:32],
                                                      SEL[:, 48:56]))
                fence("vector")
                ew("vector", lambda en: en.tensor_mul(SEL[:, 64:72], SEL[:, 56:64],
                                                      SEL[:, 0:8]))
                fence("vector", inc=sDVE)
                pe_bank(7)
                wait("tensor", sDVE)
                ew("tensor", lambda en: en.transpose(PS[7][0:8, 0:128], SEL[:, 64:72],
                                                     IDF32[:, :]), inc=sPE)
                wait("vector", sPE)
                ew("vector", lambda en, tm=tm: en.tensor_copy(
                    MFM[0:8, 128 * tm:128 * (tm + 1)], PS[7][0:8, 0:128]))
                fence("vector", inc=sDVE)
                rel_bank(7, sDVE)
            sel_done = sDVE.n
            if ffn_mode == "sel":
                return
            nexp_run = {"mbb": 0, "a1": 1, "ab1": 1, "a1slab": 1,
                        "a1mm": 1, "a1mm1": 1, "a1mmne": 1,
                        "a1slab3": 1}.get(ffn_mode, NEXP)

            ex_list = ([8] if ffn_mode == "sh" else
                       [0] if ffn_mode == "e0" else list(range(nexp_run)))
            for ex in ex_list:
                routed = ex < 8
                if routed:
                    pe_bank(4)
                    pe_bank(5)
                    wait("tensor", sDVE, sel_done)
                    for h in range(2):
                        ew("tensor", lambda en, ex=ex, h=h: en.matmul(
                            PS[4 + h][:, :], ONEH[:, 128 * ex:128 * (ex + 1)],
                            MFM[:, 512 * h:512 * (h + 1)], start=True, stop=True),
                           inc=(sPE if h == 1 else None))
                    wait("scalar", sPE)
                    for h in range(2):
                        ew("scalar", lambda en, h=h: en.activation(
                            MBB[:, 512 * h:512 * (h + 1)], PS[4 + h][:, :], AF.Copy),
                           inc=sACT)
                    rel_bank(4, sACT)
                    rel_bank(5, sACT)
                mbb_done = sACT.n
                if ffn_mode == "mbb":
                    continue

                slab_ready = {0: load_slab13(ex, 0), 1: load_slab13(ex, 1)}
                if ffn_mode == "a1slab":
                    wait("tensor", slot_sems[0], slab_ready[0])
                    wait("tensor", slot_sems[1], slab_ready[1])
                    continue
                if ffn_mode == "a1slab3":
                    wait("tensor", slot_sems[0], slab_ready[0])
                    wait("tensor", slot_sems[1], slab_ready[1])
                    slab_guard[("w13", 0)] = sPE.n
                    load_slab13(ex, 2)
                    wait("tensor", slot_sems[0])
                    continue
                ngr = 1 if ffn_mode in ("a1mm1", "a1mmne") else 4
                for mg in range(ngr):
                    slot = mg % 2
                    w1s = RB[:, 4096 * slot:4096 * (slot + 1)]
                    w3s = RB[:, 8192 + 4096 * slot:8192 + 4096 * (slot + 1)]
                    wait("tensor", slot_sems[slot], slab_ready[mg])
                    nhalf = 1 if ffn_mode in ("a1mm1", "a1mmne") else 2
                    for half in range(nhalf):
                        bi = (0, 1, 2, 3) if (2 * mg + half) % 2 == 0 else (4, 5, 6, 7)
                        for b in bi:
                            pe_bank(b)
                        for k in range(NCH):
                            for m in range(2):
                                ew("tensor", lambda en, k=k, m=m, half=half, w1s=w1s,
                                   bi=bi: en.matmul(
                                       PS[bi[m]][:, :],
                                       w1s[:, 256 * k + 128 * m:256 * k + 128 * (m + 1)],
                                       RA[:, 1024 * k + 512 * half:
                                          1024 * k + 512 * (half + 1)],
                                       start=(k == 0), stop=(k == NCH - 1)))
                                ew("tensor", lambda en, k=k, m=m, half=half, w3s=w3s,
                                   bi=bi: en.matmul(
                                       PS[bi[2 + m]][:, :],
                                       w3s[:, 256 * k + 128 * m:256 * k + 128 * (m + 1)],
                                       RA[:, 1024 * k + 512 * half:
                                          1024 * k + 512 * (half + 1)],
                                       start=(k == 0), stop=(k == NCH - 1)),
                                   inc=(sPE if (k == NCH - 1 and m == 1) else None))
                        wait("vector", sPE)
                        wait("scalar", sPE)
                        if ffn_mode == "a1mmne":
                            ew("vector", lambda en: en.engine_nop(), inc=sDVE)
                            for b in bi:
                                bank_rel[b] = None
                            continue
                        if ffn_mode in ("a1mm", "a1mm1"):
                            for m in range(2):
                                hc = 2 * mg + m
                                hdst = RE[:, 1024 * hc + 512 * half:
                                          1024 * hc + 512 * (half + 1)]
                                ew("vector", lambda en, bi=bi, m=m, hdst=hdst:
                                   en.tensor_copy(hdst, PS[bi[m]][:, :]), inc=sDVE)
                            for b in bi:
                                rel_bank(b, sDVE)
                            continue
                        if routed:
                            wait("vector", sACT, mbb_done)
                        for m in range(2):
                            hc = 2 * mg + m
                            hdst = RE[:, 1024 * hc + 512 * half:
                                      1024 * hc + 512 * (half + 1)]
                            qtm = QT[:, 512 * m:512 * (m + 1)]
                            htm = HT[:, 512 * m:512 * (m + 1)]
                            if routed:
                                ew("vector", lambda en, bi=bi, m=m, half=half, htm=htm:
                                   en.tensor_mul(htm, PS[bi[m]][:, :],
                                                 MBB[:, 512 * half:512 * (half + 1)]),
                                   inc=sDVE)
                                wait("scalar", sDVE)
                                ew("scalar", lambda en, qtm=qtm, htm=htm:
                                   en.activation(qtm, htm, AF.Silu), inc=sACT)
                                wait("vector", sACT)
                                ew("vector", lambda en, bi=bi, m=m, half=half, htm=htm:
                                   en.tensor_mul(htm, PS[bi[2 + m]][:, :],
                                                 MBB[:, 512 * half:512 * (half + 1)]))
                                ew("vector", lambda en, qtm=qtm, htm=htm, hdst=hdst:
                                   en.tensor_mul(hdst, qtm, htm), inc=sDVE)
                            else:
                                ew("scalar", lambda en, bi=bi, m=m, qtm=qtm:
                                   en.activation(qtm, PS[bi[m]][:, :], AF.Silu),
                                   inc=sACT)
                                wait("vector", sACT)
                                ew("vector", lambda en, bi=bi, m=m, qtm=qtm, hdst=hdst:
                                   en.tensor_mul(hdst, qtm, PS[bi[2 + m]][:, :]),
                                   inc=sDVE)
                        for b in bi:
                            rel_bank(b, sDVE)
                    slab_guard[("w13", slot)] = sPE.n
                    if mg + 2 < 4:
                        slab_ready[mg + 2] = load_slab13(ex, mg + 2)
                h_done = sDVE.n
                if ffn_mode in ("a1", "a1mm", "a1mm1", "a1mmne", "a1slab3"):
                    continue

                w2_ready = {0: load_slab2(ex, 0), 1: load_slab2(ex, 1)}
                for dg in range(8):
                    slot = dg % 2
                    w2s = RB[:, 16384 + 2048 * slot:16384 + 2048 * (slot + 1)]
                    wait("tensor", slot_sems[slot], w2_ready[dg])
                    wait("tensor", sDVE, h_done)
                    for half in range(2):
                        bi = (0, 1) if (2 * dg + half) % 2 == 0 else (2, 3)
                        for b in bi:
                            pe_bank(b)
                        for k in range(8):
                            for m in range(2):
                                ew("tensor", lambda en, k=k, m=m, half=half, w2s=w2s,
                                   bi=bi: en.matmul(
                                       PS[bi[m]][:, :],
                                       w2s[:, 256 * k + 128 * m:256 * k + 128 * (m + 1)],
                                       RE[:, 1024 * k + 512 * half:
                                          1024 * k + 512 * (half + 1)],
                                       start=(k == 0), stop=(k == 7)),
                                   inc=(sPE if (k == 7 and m == 1) else None))
                        wait("vector", sPE)
                        for m in range(2):
                            dc = 2 * dg + m
                            xc = X[:, 1024 * dc + 512 * half:1024 * dc + 512 * (half + 1)]
                            ew("vector", lambda en, bi=bi, m=m, xc=xc:
                               en.scalar_tensor_tensor(xc, PS[bi[m]][:, :], 0.5, xc,
                                                       ALU.mult, ALU.add), inc=sDVE)
                        for b in bi:
                            rel_bank(b, sDVE)
                    slab_guard[("w2", slot)] = sPE.n
                    if dg + 2 < 8:
                        w2_ready[dg + 2] = load_slab2(ex, dg + 2)

        def phase_qkv():
            xln_done = sDVE.n
            wait("sync", sDVE, xln_done)
            dma("sync", cc1_in.rearrange("(c p) t -> p c t", p=128),
                RA[:, :].rearrange("p (c t) -> p c t", t=TL))
            cc1_staged = sX.n
            wait("gpsimd", sX, cc1_staged)
            ew("gpsimd", lambda en: en.collective_compute(
                "AllGather", ALU.bypass, replica_groups=[list(range(NCORES))],
                ins=[cc1_in], outs=[cc1_out]), inc=sCC)
            ag1_done = sCC.n
            g0 = slab_guard.pop(("w13", 0), None)
            g1 = slab_guard.pop(("w13", 1), None)
            g2a = slab_guard.pop(("w2", 0), None)
            g2b = slab_guard.pop(("w2", 1), None)
            pe_fence = max(x for x in (g0, g1, g2a, g2b, 0) if x is not None)
            wait("sync", sPE, pe_fence)
            dma("sync", RB[0:64, 0:16384], tq_in)
            tq_done = sX.n
            wait("gpsimd", sPE, pe_fence)
            dma("gpsimd", RE[:, 0:6144].rearrange("p (k f) -> p k f", f=384),
                wqkvT_d.rearrange("(k p) f -> p k f", p=128))
            wqkv_done = sW.n

            xg_view = cc1_out.rearrange("(b k p) t -> b p k t", k=NCH, p=128)
            xg_loaded = []
            wait("sync", sCC, ag1_done)
            dma("sync", RA[:, 0:8192].rearrange("p (k t) -> p k t", t=512),
                xg_view[0, :, :, 0:512])
            xg_loaded.append(sX.n)
            pe_after_n = []
            qslot_dve = [0, 0]
            qsq_act = [0, 0]
            hx_dma = [0, 0]
            qa_q = False
            qa_k = False
            for n in range(16):
                slot = n % 2
                if n + 1 < 16:
                    if n >= 1:
                        wait("sync", sPE, pe_after_n[n - 1])
                    nn = n + 1
                    dma("sync",
                        RA[:, 8192 * (nn % 2):8192 * (nn % 2) + 8192].rearrange(
                            "p (k t) -> p k t", t=512),
                        xg_view[nn // 2, :, :, 512 * (nn % 2):512 * (nn % 2) + 512])
                    xg_loaded.append(sX.n)
                bset = (0, 1, 2) if n % 2 == 0 else (3, 4, 5)
                wait("tensor", sX, xg_loaded[n])
                if n == 0:
                    wait("tensor", sW, wqkv_done)
                for m in range(3):
                    pe_bank(bset[m])
                    for k in range(NCH):
                        ew("tensor", lambda en, m=m, k=k, slot=slot, bset=bset:
                           en.matmul(
                               PS[bset[m]][:, :],
                               RE[:, 384 * k + 128 * m:384 * k + 128 * (m + 1)],
                               RA[:, 8192 * slot + 512 * k:8192 * slot + 512 * (k + 1)],
                               start=(k == 0), stop=(k == NCH - 1)),
                           inc=(sPE if k == NCH - 1 else None))
                pe_after_n.append(sPE.n)
                wait("scalar", sPE, pe_after_n[n])
                if n == 0:
                    wait("scalar", sX, tq_done)
                    wait("vector", sX, tq_done)
                for m in range(3):
                    qslot = (3 * n + m) % 2
                    qte = QT[0:64, 512 * qslot:512 * (qslot + 1)]
                    qto = HT[0:64, 512 * qslot:512 * (qslot + 1)]
                    hx = RT[64:128, :] if qslot == 0 else HT[64:128, :]
                    if qslot_dve[qslot]:
                        wait("scalar", sDVE, qslot_dve[qslot])
                    ew("scalar", lambda en, m=m, bset=bset, qte=qte: en.activation(
                        qte, PS[bset[m]][0:64, :], AF.Copy))
                    if m < 2:
                        ew("scalar", lambda en, m=m, bset=bset, qto=qto:
                           en.activation(qto, PS[bset[m]][64:128, :], AF.Copy))
                    else:
                        ew("scalar", lambda en, n=n, m=m, bset=bset: en.activation(
                            KV[64:128, 512 * n:512 * (n + 1)],
                            PS[bset[m]][64:128, :], AF.Copy))
                    fence("scalar", inc=sACT)
                    rel_bank(bset[m], sACT)
                    wait("vector", sACT)
                    tqc = RB[0:64, 512 * n:512 * (n + 1)]
                    tqs = RB[0:64, 8192 + 512 * n:8192 + 512 * (n + 1)]
                    if m < 2:
                        if hx_dma[qslot]:
                            wait("vector", sS, hx_dma[qslot])
                        if qsq_act[qslot]:
                            wait("vector", sACT, qsq_act[qslot])
                        ew("vector", lambda en, qte=qte, tqc=tqc: en.tensor_mul(
                            RT[0:64, 0:512], qte, tqc))
                        ew("vector", lambda en, qto=qto, tqs=tqs: en.tensor_mul(
                            RT[0:64, 512:1024], qto, tqs))
                        ew("vector", lambda en, hx=hx: en.tensor_sub(
                            hx[:, 0:512], RT[0:64, 0:512], RT[0:64, 512:1024]))
                        ew("vector", lambda en, qte=qte, tqs=tqs: en.tensor_mul(
                            RT[0:64, 0:512], qte, tqs))
                        ew("vector", lambda en, qto=qto, tqc=tqc: en.tensor_mul(
                            RT[0:64, 512:1024], qto, tqc))
                        ew("vector", lambda en, hx=hx: en.tensor_add(
                            hx[:, 512:1024], RT[0:64, 0:512], RT[0:64, 512:1024]))
                        fence("vector", inc=sDVE)
                        qslot_dve[qslot] = sDVE.n
                        wait("scalar", sDVE, qslot_dve[qslot])
                        ew("scalar", lambda en, hx=hx: en.activation(
                            QT[64:128, 0:1024], hx[:, :], AF.Square))
                        fence("scalar", inc=sACT)
                        qsq_act[qslot] = sACT.n
                        pe_bank(7)
                        wait("tensor", sACT)
                        ew("tensor", lambda en: en.matmul(
                            PS[7][0:1, :], ONESC[64:128, :], QT[64:128, 0:512],
                            start=True, stop=False))
                        ew("tensor", lambda en: en.matmul(
                            PS[7][0:1, :], ONESC[64:128, :], QT[64:128, 512:1024],
                            start=False, stop=True), inc=sPE)
                        wait("vector", sPE)
                        ew("vector", lambda en: en.reduce_max(
                            QA[0:1, 1:2], PS[7][0:1, :], axis=AX.X))
                        fence("vector")
                        if not qa_q:
                            ew("vector", lambda en: en.tensor_copy(
                                QA[0:1, 0:1], QA[0:1, 1:2]))
                            qa_q = True
                        else:
                            ew("vector", lambda en: en.tensor_max(
                                QA[0:1, 0:1], QA[0:1, 0:1], QA[0:1, 1:2]))
                        fence("vector", inc=sDVE)
                        rel_bank(7, sDVE)
                        dma("scalar",
                            qp_dram[128 * m:128 * (m + 1), 512 * n:512 * (n + 1)]
                            .rearrange("(h p) t -> p h t", h=2),
                            hx[:, :].rearrange("p (h t) -> p h t", h=2))
                        hx_dma[qslot] = sS.n
                    else:
                        kvd = KV[:, 512 * n:512 * (n + 1)]
                        ew("vector", lambda en, qte=qte, tqc=tqc: en.tensor_mul(
                            RT[0:32, 0:512], qte[0:32, :], tqc[0:32, :]))
                        ew("vector", lambda en, qte=qte, tqs=tqs: en.tensor_mul(
                            RT[0:32, 512:1024], qte[32:64, :], tqs[32:64, :]))
                        ew("vector", lambda en, kvd=kvd: en.tensor_sub(
                            kvd[0:32, :], RT[0:32, 0:512], RT[0:32, 512:1024]))
                        ew("vector", lambda en, qte=qte, tqs=tqs: en.tensor_mul(
                            RT[0:32, 0:512], qte[0:32, :], tqs[0:32, :]))
                        ew("vector", lambda en, qte=qte, tqc=tqc: en.tensor_mul(
                            RT[0:32, 512:1024], qte[32:64, :], tqc[32:64, :]))
                        ew("vector", lambda en, kvd=kvd: en.tensor_add(
                            kvd[32:64, :], RT[0:32, 0:512], RT[0:32, 512:1024]))
                        fence("vector", inc=sDVE)
                        qslot_dve[qslot] = sDVE.n
                        wait("scalar", sDVE, qslot_dve[qslot])
                        ew("scalar", lambda en, kvd=kvd: en.activation(
                            QT[64:128, 0:512], kvd[0:64, :], AF.Square))
                        fence("scalar", inc=sACT)
                        qsq_act[qslot] = sACT.n
                        pe_bank(7)
                        wait("tensor", sACT)
                        ew("tensor", lambda en: en.matmul(
                            PS[7][0:1, :], ONESC[64:128, :], QT[64:128, 0:512],
                            start=True, stop=True), inc=sPE)
                        wait("vector", sPE)
                        ew("vector", lambda en: en.reduce_max(
                            QA[0:1, 3:4], PS[7][0:1, :], axis=AX.X))
                        fence("vector")
                        if not qa_k:
                            ew("vector", lambda en: en.tensor_copy(
                                QA[0:1, 2:3], QA[0:1, 3:4]))
                            qa_k = True
                        else:
                            ew("vector", lambda en: en.tensor_max(
                                QA[0:1, 2:3], QA[0:1, 2:3], QA[0:1, 3:4]))
                        fence("vector", inc=sDVE)
                        rel_bank(7, sDVE)
            rope_all_done = sDVE.n
            ew("vector", lambda en: en.tensor_mul(UROW[:, 202:203],
                                                  QA[0:1, 0:1], QA[0:1, 2:3]))
            fence("vector", inc=sDVE)
            wait("scalar", sDVE)
            ew("scalar", lambda en: en.activation(UROW[:, 202:203],
                                                  UROW[:, 202:203], AF.Ln))
            fence("scalar")
            ew("scalar", lambda en: en.activation(UROW[:, 202:203],
                                                  UROW[:, 202:203], AF.Exp,
                                                  scale=0.5))
            fence("scalar", inc=sACT)
            wait("vector", sACT)
            ew("vector", lambda en: en.tensor_single_scalar(
                UROW[:, 203:204], UROW[:, 202:203], -0.125, ALU.mult))
            fence("vector", inc=sDVE)
            u_row_done = sDVE.n
            pe_bank(6)
            wait("tensor", sDVE, u_row_done)
            ew("tensor", lambda en: en.matmul(PS[6][:, 0:1], ONES1[:, :],
                                              UROW[:, 203:204],
                                              start=True, stop=True), inc=sPE)
            wait("vector", sPE)
            ew("vector", lambda en: en.tensor_copy(NEGUB[:, :], PS[6][:, 0:1]))
            fence("vector", inc=sDVE)
            rel_bank(6, sDVE)

        def phase_attn():
            kv_done = sDVE.n
            qp_stores = sS.n
            ew("vector", lambda en: en.memset(VAUG[:, :], 1.0), inc=sDVE)
            vaug_ones = sDVE.n
            for kt in range(64):
                b = 4 + kt % 2
                pe_bank(b)
                if kt == 0:
                    wait("tensor", sDVE, kv_done)
                ew("tensor", lambda en, kt=kt, b=b: en.transpose(
                    PS[b].bitcast(BF16)[:, 0:64],
                    KV[64:128, 128 * kt:128 * (kt + 1)],
                    IDBF[64:128, 64:128]), inc=sPE)
                wait("vector", sPE)
                if kt == 0:
                    wait("vector", sDVE, vaug_ones)
                ew("vector", lambda en, kt=kt, b=b: en.tensor_copy(
                    VAUG[:, 65 * kt:65 * kt + 64],
                    PS[b].bitcast(BF16)[:, 0:64]), inc=sDVE)
                rel_bank(b, sDVE)
            wait("vector", sPE)
            ew("vector", lambda en: en.tensor_copy(KV[64:128, :], KV[0:64, :]))
            fence("vector", inc=sDVE)
            vaug_done = sDVE.n

            qext = RE[0:64, :]
            pt_slots = [QT[:, 0:1024], HT[:, 0:1024]]
            wait("scalar", sS, qp_stores)
            wait("vector", sS, qp_stores)
            y_store = []
            for h in range(HLOC):
                p = h // 2
                isB = h % 2 == 1
                wait("sync", sPE)
                if h == 0:
                    wait("sync", sS, qp_stores)
                re_ = 128 * p + (32 if isB else 0)
                ro_ = 128 * p + 64 + (32 if isB else 0)
                dma("sync", RE[0:32, :], qp_dram[re_:re_ + 32, :])
                dma("sync", RE[32:64, :], qp_dram[ro_:ro_ + 32, :])
                dma("sync", RE[64:96, :], qp_dram[re_:re_ + 32, :])
                dma("sync", RE[96:128, :], qp_dram[ro_:ro_ + 32, :])
                q_ready = sX.n
                wait("tensor", sX, q_ready)
                if h == 0:
                    wait("tensor", sDVE, vaug_done)
                praw_dve = 0
                pts = [QT[:, 0:512], QT[:, 512:1024], HT[:, 0:512],
                       HT[:, 512:1024]]
                for qc in range(16):
                    accb = 6 + qc % 2
                    ktmax = 4 * qc + 4
                    pe_bank(accb)
                    av_pe = []
                    s_pe = []

                    def emit_s(kt, qc=qc):
                        sb = kt % 4
                        par = 64 * (kt % 2)
                        pe_bank(sb)
                        ew("tensor", lambda en, kt=kt, qc=qc, sb=sb, par=par:
                           en.matmul(
                               PS[sb],
                               KV[par:par + 64, 128 * kt:128 * (kt + 1)],
                               RE[par:par + 64, 512 * qc:512 * (qc + 1)],
                               start=True, stop=True), inc=sPE)
                        s_pe.append(sPE.n)

                    emit_s(0)
                    emit_s(1)
                    for kt in range(ktmax):
                        sb = kt % 4
                        diag = kt >= 4 * qc
                        wait("scalar", sPE, s_pe[kt])
                        if kt >= 4:
                            wait("scalar", sPE, av_pe[kt - 4])
                        if diag and praw_dve:
                            wait("scalar", sDVE, praw_dve)
                        dst = RT[:, 0:512] if diag else pts[sb]
                        ew("scalar", lambda en, sb=sb, dst=dst: en.activation(
                            dst, PS[sb], AF.Exp, bias=NEGUB[:, 0:1],
                            scale=0.125), inc=sACT)
                        rel_bank(sb, sACT)
                        if diag:
                            r = kt - 4 * qc
                            wait("vector", sACT)
                            if kt >= 4:
                                wait("vector", sPE, av_pe[kt - 4])
                            ew("vector", lambda en, sb=sb, r=r: en.tensor_mul(
                                pts[sb], RT[:, 0:512],
                                M01S[:, 512 * r:512 * (r + 1)]), inc=sDVE)
                            praw_dve = sDVE.n
                        if kt % 2 == 0:
                            if kt + 2 < ktmax:
                                emit_s(kt + 2)
                            if kt + 3 < ktmax:
                                emit_s(kt + 3)
                        if diag:
                            wait("tensor", sDVE, praw_dve)
                        else:
                            wait("tensor", sACT)
                        ew("tensor", lambda en, kt=kt, sb=sb, accb=accb, ktmax=ktmax:
                           en.matmul(
                               PS[accb][0:65, :], VAUG[:, 65 * kt:65 * kt + 65],
                               pts[sb],
                               start=(kt == 0), stop=(kt == ktmax - 1),
                               skip_group_check=True), inc=sPE)
                        av_pe.append(sPE.n)
                    wait("scalar", sPE, av_pe[-1])
                    ew("scalar", lambda en, accb=accb: en.activation(
                        ACCS[0:65, :], PS[accb][0:65, :], AF.Copy))
                    fence("scalar", inc=sACT)
                    rel_bank(accb, sACT)
                    wait("vector", sACT)
                    ew("vector", lambda en: en.reciprocal(REC, ACCS[64:65, :]))
                    fence("vector", inc=sDVE)
                    pe_bank(2)
                    wait("tensor", sDVE)
                    ew("tensor", lambda en: en.matmul(PS[2][0:64, :],
                                                      ONES1[0:1, 0:64], REC,
                                                      start=True, stop=True), inc=sPE)
                    yslot = qc % 2
                    if len(y_store) >= 2:
                        wait("vector", sS, y_store[-2])
                    wait("vector", sPE)
                    ew("vector", lambda en, yslot=yslot: en.tensor_mul(
                        YO[:, 512 * yslot:512 * (yslot + 1)], ACCS[0:64, :],
                        PS[2][0:64, :]), inc=sDVE)
                    rel_bank(2, sDVE)
                    wait("scalar", sDVE)
                    dma("scalar", cc2_in[64 * h:64 * (h + 1), 512 * qc:512 * (qc + 1)],
                        YO[:, 512 * yslot:512 * (yslot + 1)])
                    y_store.append(sS.n)
            wait("gpsimd", sS, y_store[-1])
            ew("gpsimd", lambda en: en.collective_compute(
                "AllGather", ALU.bypass, replica_groups=[list(range(NCORES))],
                ins=[cc2_in], outs=[cc2_out]), inc=sCC)

        def phase_wo():
            ag2_done = sCC.n
            wait("gpsimd", sCC, ag2_done)
            wait("gpsimd", sPE)
            src = cc2_out.rearrange("f (b t) -> (f b) t", b=NCORES)
            gdone = []
            for k in range(NCH):
                ew("gpsimd", lambda en, k=k: en.indirect_dma_start(
                    out=RA[:, 1024 * k:1024 * (k + 1)],
                    out_offset=None,
                    in_=src,
                    in_offset=IndirectOffsetOnAxis(WOIDX[:, k:k + 1], 0)),
                   inc=sW, k=16)
                gdone.append(sW.n)
            wo_ready = {}
            for mg in range(2):
                slot = mg % 2
                dma("gpsimd",
                    RB[:, 4096 * slot:4096 * (slot + 1)].rearrange(
                        "p (k m) -> p k m", m=256),
                    woT_d[:, 256 * mg:256 * (mg + 1)].rearrange(
                        "(k p) m -> p k m", p=128), sc=slot_sems[slot])
                wo_ready[mg] = slot_sems[slot].n
            for mg in range(8):
                slot = mg % 2
                wos = RB[:, 4096 * slot:4096 * (slot + 1)]
                wait("tensor", slot_sems[slot], wo_ready[mg])
                wait("tensor", sW, gdone[-1])
                for half in range(2):
                    bi = (0, 1) if (2 * mg + half) % 2 == 0 else (2, 3)
                    for b in bi:
                        pe_bank(b)
                    for k in range(NCH):
                        for m in range(2):
                            ew("tensor", lambda en, k=k, m=m, half=half, wos=wos,
                               bi=bi: en.matmul(
                                   PS[bi[m]][:, :],
                                   wos[:, 256 * k + 128 * m:256 * k + 128 * (m + 1)],
                                   RA[:, 1024 * k + 512 * half:
                                      1024 * k + 512 * (half + 1)],
                                   start=(k == 0), stop=(k == NCH - 1)),
                               inc=(sPE if (k == NCH - 1 and m == 1) else None))
                    wait("vector", sPE)
                    for m in range(2):
                        dc = 2 * mg + m
                        xc = X[:, 1024 * dc + 512 * half:1024 * dc + 512 * (half + 1)]
                        ew("vector", lambda en, bi=bi, m=m, xc=xc: en.tensor_add(
                            xc, xc, PS[bi[m]][:, :]), inc=sDVE)
                    for b in bi:
                        rel_bank(b, sDVE)
                if mg + 2 < 8:
                    wait("gpsimd", sPE)
                    dma("gpsimd",
                        RB[:, 4096 * slot:4096 * (slot + 1)].rearrange(
                            "p (k m) -> p k m", m=256),
                        woT_d[:, 256 * (mg + 2):256 * (mg + 3)].rearrange(
                            "(k p) m -> p k m", p=128), sc=slot_sems[slot])
                    wo_ready[mg + 2] = slot_sems[slot].n
            slab_guard[("w13", 0)] = sPE.n
            slab_guard[("w13", 1)] = sPE.n
            slab_guard[("w2", 0)] = sPE.n
            slab_guard[("w2", 1)] = sPE.n

        stop = PHASES.index(upto)
        phase_fns = {"ln1": phase_ln, "ffn1": phase_ffn, "ln2": phase_ln,
                     "qkv": phase_qkv, "attn": phase_attn, "wo": phase_wo,
                     "ln3": phase_ln, "ffn2": phase_ffn}
        for i, name in enumerate(PHASES):
            if i > stop:
                break
            phase_fns[name]()

        if "xln" in tap_aps or "tap_xln" in tap_aps:
            pass
        if "tap_xln" in tap_aps:
            wait("sync", sDVE)
            dma("sync", tap_aps["tap_xln"], RA[:, :])
        if "tap_re" in tap_aps:
            wait("sync", sDVE)
            dma("sync", tap_aps["tap_re"], RE[:, :])
        if "tap_mfm" in tap_aps:
            wait("sync", sDVE)
            dma("sync", tap_aps["tap_mfm"], MFM[:, :])
        if "tap_negub" in tap_aps:
            wait("sync", sDVE)
            dma("sync", tap_aps["tap_negub"], NEGUB[:, :])
            dma("sync", tap_aps["tap_urow"], ROWS[0:1, 512:1024])
        if "tap_mbb" in tap_aps:
            wait("sync", sACT)
            dma("sync", tap_aps["tap_mbb"], MBB[:, :])
        if "tap_kv" in tap_aps:
            wait("sync", sDVE)
            dma("sync", tap_aps["tap_kv"], KV[:, :])
        if "tap_vaug" in tap_aps:
            wait("sync", sDVE)
            dma("sync", tap_aps["tap_vaug"], VAUG[:, :])
        if "tap_qp" in tap_aps:
            wait("sync", sS)
            wait("sync", sDVE)
            for half in range(4):
                dma("sync", RA[:, 0:4096], qp_dram[:, :].rearrange(
                    "(a p) t -> p a t", p=128)[:, :, 2048 * half:2048 * (half + 1)])
                wait("sync", sX)
                dma("sync", tap_aps["tap_qp"].rearrange("(a p) t -> p a t", p=128)
                    [:, :, 2048 * half:2048 * (half + 1)], RA[:, 0:4096])
                wait("sync", sX)
        if "tap_cc2" in tap_aps:
            wait("sync", sCC)
            for half in range(4):
                dma("sync", RA[:, 0:4096], cc2_in.rearrange(
                    "(a p) t -> p a t", p=128)[:, :, 2048 * half:2048 * (half + 1)])
                wait("sync", sX)
                dma("sync", tap_aps["tap_cc2"].rearrange("(a p) t -> p a t", p=128)
                    [:, :, 2048 * half:2048 * (half + 1)], RA[:, 0:4096])
                wait("sync", sX)

        wait("sync", sDVE)
        wait("sync", sPE)
        dma("sync", out_x.rearrange("(c p) t -> p c t", p=128),
            X[:, :].rearrange("p (c t) -> p c t", t=TL))
        wait("sync", sX)

        @block.sync
        def _(en):
            for f in prog["sync"]:
                f(en)

        @block.scalar
        def _(en):
            for f in prog["scalar"]:
                f(en)

        @block.vector
        def _(en):
            for f in prog["vector"]:
                f(en)

        @block.tensor
        def _(en):
            for f in prog["tensor"]:
                f(en)

        @block.gpsimd
        def _(en):
            for f in prog["gpsimd"]:
                f(en)

    return nc



def host_prep(inputs):
    x = np.asarray(inputs["L0_attn_x"], dtype=np.float32).reshape(T, D)
    wq = np.asarray(inputs["L0_attn_wq"], dtype=np.float32)
    wk = np.asarray(inputs["L0_attn_wk"], dtype=np.float32)
    wv = np.asarray(inputs["L0_attn_wv"], dtype=np.float32)
    wo = np.asarray(inputs["L0_attn_wo"], dtype=np.float32)
    eb = np.asarray(inputs["L0_ffn_expert_bias"], dtype=np.float32)
    gw = np.asarray(inputs["L0_ffn_gate_w"], dtype=np.float32)
    w1 = np.asarray(inputs["L0_ffn_w1"], dtype=np.float32)
    w2 = np.asarray(inputs["L0_ffn_w2"], dtype=np.float32)
    w3 = np.asarray(inputs["L0_ffn_w3"], dtype=np.float32)
    sw1 = np.asarray(inputs["L0_ffn_sw1"], dtype=np.float32)
    sw2 = np.asarray(inputs["L0_ffn_sw2"], dtype=np.float32)
    sw3 = np.asarray(inputs["L0_ffn_sw3"], dtype=np.float32)

    C = np.ascontiguousarray

    w1T = np.stack([C(w1[i].T) for i in range(8)] + [C(sw1.T)])
    w3T = np.stack([C(w3[i].T) for i in range(8)] + [C(sw3.T)])
    w2T = np.stack([C(w2[i].T) for i in range(8)] + [C(sw2.T)])
    gateT = C(gw.T)
    biasT = np.tile(eb[None, :], (128, 1)).astype(np.float32)
    woT = C(wo.T)

    theta = 1.0 / (10000.0 ** (np.arange(0, HD, 2, dtype=np.float32) / HD))
    ang = np.arange(T, dtype=np.float32)[:, None] * theta[None, :]
    cosT = np.cos(ang).T.astype(np.float32)
    sinT = np.sin(ang).T.astype(np.float32)
    cos2 = np.concatenate([cosT, cosT], axis=0)
    sin2 = np.concatenate([sinT, sinT], axis=0)
    tq = np.concatenate([cos2, sin2], axis=1).astype(BF)

    kk = np.arange(128)[:, None]
    qq = np.arange(512)[None, :]
    m01 = np.concatenate([(kk <= qq - 128 * r) for r in range(4)],
                         axis=1).astype(BF)

    idbf = np.eye(128, dtype=BF)
    idf32 = np.eye(128, dtype=np.float32)
    ones1 = np.ones((1, 128), dtype=np.float32)
    onescol = np.ones((128, 1), dtype=BF)
    oneh = np.zeros((8, 1024), dtype=BF)
    for ex in range(8):
        oneh[ex, 128 * ex:128 * (ex + 1)] = 1.0

    per_core = []
    for c in range(NCORES):
        xs = C(x[TL * c:TL * (c + 1)].T)
        rows = []
        for pair in range(2):
            hA = 4 * c + 2 * pair
            hB = hA + 1
            qa = wq[HD * hA:HD * (hA + 1)]
            qb = wq[HD * hB:HD * (hB + 1)]
            rows += [qa[0::2], qb[0::2], qa[1::2], qb[1::2]]
        kc = wk[HD * c:HD * (c + 1)]
        rows += [kc[0::2], kc[1::2], wv[HD * c:HD * (c + 1)]]
        wqkvT = C(np.concatenate(rows, axis=0).T)
        woidx = ((128 * np.arange(NCH)[None, :] + np.arange(128)[:, None]) * NCORES
                 + c).astype(np.int32)
        per_core.append({
            "x_in": xs, "wqkvT": wqkvT, "woT": woT,
            "w1T": w1T, "w3T": w3T, "w2T": w2T,
            "gateT": gateT, "biasT": biasT,
            "tq_in": tq, "m01_in": m01,
            "idbf_in": idbf, "idf32_in": idf32, "ones_in": ones1,
            "onescol_in": onescol, "oneh_in": oneh, "woidx_in": woidx,
        })
    return per_core


def kernel(**inputs):
    per_core = host_prep(inputs)
    nc = build_nc()
    res = run_bass_kernel_spmd(nc, per_core, core_ids=list(range(NCORES)))
    outs = [res.results[c]["out_x"] for c in range(NCORES)]
    full = np.concatenate([o.T for o in outs], axis=0)
    return full.reshape(1, T, D).astype(np.float32)
